# version 05
# baseline (speedup 1.0000x reference)
import numpy as np
import ml_dtypes
from contextlib import ExitStack

import concourse.bass as bass
import concourse.mybir as mybir
from concourse.bass_utils import run_bass_kernel_spmd

F32 = mybir.dt.float32
BF16 = mybir.dt.bfloat16
AF = mybir.ActivationFunctionType
ALU = mybir.AluOpType
AX = mybir.AxisListType
TS = bass.ts

D = 1024
NSEQ = 2
N = 2048
NTOK = NSEQ * N
NCH = N // 128
D_HEAD = 64
NHEADS = 16
M = 266
EPS = 1e-4
RATIO = M ** -0.5
DN = D_HEAD ** -0.25
LNRATIO = float(np.log(RATIO))
MCH = [(0, 128), (128, 256), (256, 266)]
H = 2048
E = 8
LN_EPS = 1e-5


class Prog:
    ENGS = ("sync", "gpsimd", "tensor", "vector", "scalar")

    def __init__(self, nc, stack):
        self.nc = nc
        self.ops = {e: [] for e in self.ENGS}
        self.sems = {e: stack.enter_context(nc.semaphore(f"sem_{e}")) for e in self.ENGS}
        self.cnt = {e: 0 for e in self.ENGS}
        self.pending = None

    def op(self, eng, fn, wait=(), inc=0, dma=False):
        if dma:
            inc = 16
        waits = [w for w in wait if w is not None]
        if self.pending is not None:
            waits = self.pending + waits
            if not hasattr(self, "_seen"):
                self._seen = set()
            if eng not in self._seen:
                self._seen.add(eng)
                if self._seen == set(self.ENGS):
                    self.pending = None
                    del self._seen
        tok = None
        if inc:
            self.cnt[eng] += inc
            tok = (eng, self.cnt[eng])
        self.ops[eng].append((tuple(waits), fn, inc))
        return tok

    def barrier(self):
        toks = [(e, self.cnt[e]) for e in self.ENGS if self.cnt[e] > 0]
        if self.pending is not None:
            pass
        self.pending = toks
        if hasattr(self, "_seen"):
            del self._seen

    def emit(self, block):
        for e in self.ENGS:
            entries = self.ops[e]
            if not entries:
                continue
            sems = self.sems

            def body(engine, entries=entries, e=e):
                waited = {}
                for waits, fn, inc in entries:
                    for (we, wv) in waits:
                        if waited.get(we, 0) >= wv:
                            continue
                        engine.wait_ge(sems[we], wv)
                        waited[we] = wv
                    inst = fn(engine)
                    if inc:
                        inst.then_inc(sems[e], inc)
            getattr(block, e)(body)


def build_program():
    nc = bass.Bass("TRN2", target_bir_lowering=False)
    xT = nc.dram_tensor("xT", [NSEQ, D, N], F32, kind="ExternalInput").ap()
    Wq = nc.dram_tensor("Wq", [D, D], F32, kind="ExternalInput").ap()
    Wk = nc.dram_tensor("Wk", [D, D], F32, kind="ExternalInput").ap()
    Wv = nc.dram_tensor("Wv", [D, D], F32, kind="ExternalInput").ap()
    Wo = nc.dram_tensor("Wo", [D, D], F32, kind="ExternalInput").ap()
    projT = nc.dram_tensor("projT", [D_HEAD, M], F32, kind="ExternalInput").ap()
    bo_l = nc.dram_tensor("bo_l", [128, 8], F32, kind="ExternalInput").ap()
    W1 = nc.dram_tensor("W1", [E, D, 2 * H], F32, kind="ExternalInput").ap()
    b1_l = nc.dram_tensor("b1_l", [128, E, 32], F32, kind="ExternalInput").ap()
    W2 = nc.dram_tensor("W2", [E, H, D], F32, kind="ExternalInput").ap()
    b2 = nc.dram_tensor("b2", [E, D], F32, kind="ExternalInput").ap()
    Wr = nc.dram_tensor("Wr", [D, E], F32, kind="ExternalInput").ap()
    br_b = nc.dram_tensor("br_b", [128, E], F32, kind="ExternalInput").ap()
    idf = nc.dram_tensor("idf", [128, 128], F32, kind="ExternalInput").ap()
    oh8 = nc.dram_tensor("oh8", [E, E * 128], BF16, kind="ExternalInput").ap()
    outT = nc.dram_tensor("outT", [NSEQ, D, N], F32, kind="ExternalOutput").ap()
    y1f = nc.dram_tensor("y1f", [D, NTOK], F32, kind="Internal").ap()
    y1bd = nc.dram_tensor("y1bd", [D, NTOK], BF16, kind="Internal").ap()
    actd = nc.dram_tensor("actd", [E, H, NTOK], BF16, kind="Internal").ap()

    stack = ExitStack()
    sbt = lambda nm, sh, dt: stack.enter_context(nc.sbuf_tensor(nm, sh, dt))
    with stack:
        P = Prog(nc, stack)
        op = P.op

        id_f = sbt("id_f", [128, 128], F32)
        ones_rf = sbt("ones_rf", [1, 128], F32)
        ones_cb = sbt("ones_cb", [128, 1], BF16)
        half64 = sbt("half64", [64, 1], BF16)
        pj_sb = sbt("pj_sb", [128, M], BF16)
        bo_sb = sbt("bo_sb", [128, 8], F32)
        b1_sb = sbt("b1_sb", [128, E, 32], F32)
        br_sb = sbt("br_sb", [128, E], F32)
        b2_sb = sbt("b2_sb", [E, D], BF16)
        wr_sb = sbt("wr_sb", [128, 8, E], BF16)
        oh_sb = sbt("oh_sb", [E, E * 128], BF16)
        eps_sb = sbt("eps_sb", [1, 1], F32)
        negf = sbt("negf", [1, 128], F32)
        onesb_r = sbt("onesb_r", [1, 512], BF16)

        bank = [stack.enter_context(nc.psum_tensor(f"bank{i}", [128, 512], F32))
                for i in range(8)]

        t_id = op("sync", lambda s: s.dma_start(out=id_f[:], in_=idf), dma=True)
        t_bo = op("sync", lambda s: s.dma_start(out=bo_sb[:], in_=bo_l), dma=True)
        t_b1 = op("sync", lambda s: s.dma_start(out=b1_sb[:], in_=b1_l), dma=True)
        t_br = op("sync", lambda s: s.dma_start(out=br_sb[:], in_=br_b), dma=True)
        t_oh = op("sync", lambda s: s.dma_start(out=oh_sb[:], in_=oh8), dma=True)
        t_pj0 = op("gpsimd", lambda g: g.dma_start(out=pj_sb[0:64, :], in_=projT), dma=True)
        t_pj = op("gpsimd", lambda g: g.dma_start(out=pj_sb[64:128, :], in_=projT), dma=True)
        t_b2 = op("gpsimd", lambda g: g.dma_start(out=b2_sb[:], in_=b2), dma=True)
        t_wr = op("gpsimd", lambda g: g.dma_start(out=wr_sb[:], in_=Wr.rearrange("(c p) e -> p c e", p=128)), dma=True)
        t_c1 = op("vector", lambda v: v.memset(ones_rf[:], 1.0), inc=1)
        t_c2 = op("vector", lambda v: v.memset(ones_cb[:], 1.0), inc=1)
        t_c3 = op("vector", lambda v: v.memset(half64[:], 0.5), inc=1)
        t_c4 = op("vector", lambda v: v.memset(eps_sb[:], LN_EPS), inc=1)
        t_c5 = op("vector", lambda v: v.memset(negf[:], -1.0), inc=1)
        t_c7 = op("vector", lambda v: v.memset(onesb_r[:], 1.0), inc=1)

        with nc.sbuf_tensor("wk_sb", [128, 8, D], BF16) as wk_sb, \
             nc.sbuf_tensor("wq_sb", [128, 8, D], BF16) as wq_sb, \
             nc.sbuf_tensor("wvo_sb", [128, 8, D], BF16) as wvo_sb, \
             nc.sbuf_tensor("attnT", [128, 8, N], BF16) as attnT:

            t_wk = op("gpsimd", lambda g: g.dma_start(out=wk_sb[:], in_=Wk.rearrange("(c p) f -> p c f", p=128)), dma=True)
            t_wq = op("gpsimd", lambda g: g.dma_start(out=wq_sb[:], in_=Wq.rearrange("(c p) f -> p c f", p=128)), dma=True)

            prev_seq_term = None
            for s in range(NSEQ):
                t_wv = op("gpsimd", lambda g: g.dma_start(out=wvo_sb[:], in_=Wv.rearrange("(c p) f -> p c f", p=128)),
                          wait=[prev_seq_term], dma=True)

                inner1 = ExitStack()
                _i = lambda nm, sh, dt: inner1.enter_context(nc.sbuf_tensor(f"{nm}_{s}", sh, dt))
                xb = _i("xb", [128, 8, N], BF16)
                vaug = _i("vaug", [128, NCH, 16 * 65], BF16)
                kT2 = _i("kT2", [128, N], BF16)
                qT2 = _i("qT2", [128, N], BF16)
                ksq = _i("ksq", [64, N], BF16)
                dash2 = _i("dash2", [128, 2, NCH, M], BF16)
                diagA_sb = _i("diagA_sb", [128, 2, NCH], F32)
                diagB_sb = _i("diagB_sb", [128, 2, NCH], F32)
                stabacc = _i("stabacc", [128, 4], F32)
                scr12 = _i("scr12", [128, 2], F32)
                nstab_r = _i("nstab_r", [1, 2], F32)
                nstab = _i("nstab", [1, 2], F32)
                nstab_b = _i("nstab_b", [128, 2], F32)
                biasb = _i("biasb", [128, 2, NCH], F32)
                kpb = [_i(f"kpb{i}", [128, M], BF16) for i in range(4)]
                ctx_all = _i("ctx_all", [128, 2, 3, 65], BF16)
                ctokb = _i("ctokb", [128, 2, 4], F32)
                crow2 = _i("crow2", [1, 2, 512], F32)
                epsv = _i("epsv", [1, 16 * 65], BF16)
                csx = _i("csx", [1, 2, 65], BF16)
                qpe = _i("qpe", [128, 512], F32)
                qpt2 = _i("qpt2", [128, 2, 3, 512], BF16)
                dinv2 = _i("dinv2", [1, 2, 512], F32)
                attf2 = _i("attf2", [64, 2, 512], F32)
                kp = kpb
                vre = vaug[:].rearrange("p c (h e) -> p c h e", e=65)

                t_xb = op("gpsimd",
                          lambda g, s=s: g.dma_start(out=xb[:], in_=xT[s].rearrange("(c p) t -> p c t", p=128)),
                          wait=[prev_seq_term], dma=True)
                t_vone = op("vector", lambda v: v.memset(vre[:, :, :, 64:65], 1.0),
                            wait=[prev_seq_term], inc=1)

                rd = {i: [] for i in range(8)}

                def mmop(banks, fn, wait=()):
                    w = list(wait)
                    for b in banks:
                        w += rd[b]
                        rd[b] = []
                    return op("tensor", fn, wait=w, inc=1)

                def rdop(eng, banks, fn, wait=(), inc=1):
                    tok = op(eng, fn, wait=list(wait), inc=inc)
                    for b in banks:
                        rd[b].append(tok)
                    return tok

                v_ev = {}
                for c in range(NCH):
                    for half in range(2):
                        bk = 2 + (c % 2) * 2 + half
                        def vmm(t, c=c, half=half, bk=bk):
                            for dc in range(8):
                                mm = t.matmul(bank[bk][:], xb[:, dc, TS(c, 128)], wvo_sb[:, dc, TS(half, 512)],
                                              start=(dc == 0), stop=(dc == 7))
                            return mm
                        tok_mm = mmop([bk], vmm, wait=[t_xb, t_wv])
                        def vev(sc, c=c, half=half, bk=bk):
                            dst = vre[:, c, TS(half, 8), 0:64]
                            src = bank[bk][:].rearrange("p (h e) -> p h e", e=64)
                            return sc.activation(out=dst, in_=src, func=AF.Copy)
                        v_ev[(c, half)] = rdop("scalar", [bk], vev, wait=[tok_mm, t_vone])
                v_done = v_ev[(NCH - 1, 1)]

                def csf(t):
                    for c in range(NCH):
                        t.matmul(bank[2][0:1, :], ones_cb[:], vaug[:, c, 0:512],
                                 start=(c == 0), stop=(c == NCH - 1))
                        t.matmul(bank[3][0:1, :], ones_cb[:], vaug[:, c, 512:1024],
                                 start=(c == 0), stop=(c == NCH - 1))
                        mm = t.matmul(bank[4][0:1, 0:16], ones_cb[:], vaug[:, c, 1024:1040],
                                      start=(c == 0), stop=(c == NCH - 1))
                    return mm
                cs_mm = mmop([2, 3, 4], csf, wait=[v_done, t_c2])
                def cse(sc):
                    sc.activation(out=epsv[0:1, 0:512], in_=bank[2][0:1, :], func=AF.Copy, scale=RATIO * EPS)
                    sc.activation(out=epsv[0:1, 512:1024], in_=bank[3][0:1, :], func=AF.Copy, scale=RATIO * EPS)
                    return sc.activation(out=epsv[0:1, 1024:1040], in_=bank[4][0:1, 0:16], func=AF.Copy,
                                         scale=RATIO * EPS)
                epsv_tok = rdop("scalar", [2, 3, 4], cse, wait=[cs_mm])

                prev_dgm = None
                prev_exp = None
                prev_aff = None
                prev_ctx = {}
                csx_toks = [None, None]
                prev_at = {0: None, 1: None}
                prev_ae = {0: None, 1: None}
                prev_amul = {0: None, 1: None}
                seq_term = None

                for hp in range(8):
                    def proj_pair(w_sb, dst, hp=hp):
                        evs = []
                        for j in range(4):
                            bk = 6 + (j % 2)
                            def mmf(t, j=j, w_sb=w_sb, hp=hp, bk=bk):
                                for dc in range(8):
                                    mm = t.matmul(bank[bk][:], w_sb[:, dc, TS(hp, 128)],
                                                  xb[:, dc, TS(j, 512)], start=(dc == 0), stop=(dc == 7))
                                return mm
                            tok_mm = mmop([bk], mmf, wait=[t_xb, t_wk, t_wq])
                            evs.append(rdop("scalar", [bk],
                                            lambda sc, j=j, dst=dst, bk=bk: sc.activation(
                                                out=dst[:, TS(j, 512)], in_=bank[bk][:], func=AF.Copy, scale=DN),
                                            wait=[tok_mm]))
                        return evs

                    k_ev = proj_pair(wk_sb, kT2)
                    q_ev = proj_pair(wq_sb, qT2)

                    ctxev_toks = [None, None]
                    dgmA_toks = [None, None]
                    biasA_last = [None, None]
                    for h2 in range(2):
                        b0p = 64 * h2
                        hd = 2 * hp + h2
                        t_k2 = op("scalar",
                                  lambda sc, b0p=b0p: sc.activation(out=ksq[:], in_=kT2[b0p:b0p + 64, :],
                                                                    func=AF.Square),
                                  wait=[k_ev[-1], prev_dgm], inc=1)
                        def dgmf(t):
                            for c in range(NCH):
                                mm = t.matmul(bank[6][:, c:c + 1], ksq[:, TS(c, 128)], half64[:],
                                              start=True, stop=True)
                            return mm
                        dgm = mmop([6], dgmf, wait=[t_k2, t_c3])
                        prev_dgm = dgm
                        dgmA_toks[h2] = dgm
                        dcopy = rdop("vector", [6],
                                     lambda v, h2=h2: v.tensor_copy(out=diagA_sb[:, h2, :], in_=bank[6][:, 0:NCH]),
                                     wait=[dgm])
                        ev_tok = [None] * NCH
                        for c in range(NCH):
                            dmm = mmop([c % 2],
                                       lambda t, c=c, b0p=b0p: t.matmul(
                                           bank[c % 2][:, 0:M], kT2[b0p:b0p + 64, TS(c, 128)],
                                           pj_sb[b0p:b0p + 64, :], start=True, stop=True),
                                       wait=[k_ev[c // 4], t_pj])
                            ev_tok[c] = rdop("vector", [c % 2],
                                             lambda v, c=c, h2=h2: v.tensor_copy(
                                                 out=dash2[:, h2, c, :], in_=bank[c % 2][:, 0:M]),
                                             wait=[dmm])
                        st_all = op("vector",
                                    lambda v, h2=h2: v.reduce_max(stabacc[:, 2 * h2:2 * h2 + 1],
                                                                  dash2[:, h2, :, :], axis=AX.XY),
                                    wait=[ev_tok[NCH - 1]], inc=1)
                        tp = mmop([6],
                                  lambda t, h2=h2: t.transpose(bank[6][0:1, 32:160],
                                                               stabacc[:, 2 * h2:2 * h2 + 1],
                                                               id_f[:]),
                                  wait=[st_all, t_id])
                        nstr = rdop("vector", [6],
                                    lambda v, h2=h2: v.reduce_max(nstab_r[0:1, h2:h2 + 1], bank[6][0:1, 32:160],
                                                                  axis=AX.X, negate=True),
                                    wait=[tp])
                        nst = op("vector",
                                 lambda v, h2=h2: v.tensor_scalar_add(nstab[0:1, h2:h2 + 1], nstab_r[0:1, h2:h2 + 1],
                                                                      LNRATIO),
                                 wait=[nstr], inc=1)
                        nstbc = mmop([6],
                                     lambda t, h2=h2: t.matmul(bank[6][:, 20:21], ones_rf[:], nstab[0:1, h2:h2 + 1],
                                                               start=True, stop=True),
                                     wait=[nst, t_c1])
                        nstb_ev = rdop("vector", [6],
                                       lambda v, h2=h2: v.tensor_copy(out=nstab_b[:, h2:h2 + 1], in_=bank[6][:, 20:21]),
                                       wait=[nstbc])
                        bias_all = op("vector",
                                      lambda v, h2=h2: v.tensor_scalar(
                                          biasb[:, h2, :], diagA_sb[:, h2, :], -1.0,
                                          nstab_b[:, h2:h2 + 1], ALU.mult, ALU.add),
                                      wait=[dcopy, nstb_ev, prev_exp], inc=1)
                        exp_tok = [None] * NCH
                        ctx_tok = [None] * NCH
                        for c in range(NCH):
                            ew = [bias_all, prev_ctx.get(c % 4)]
                            exp_tok[c] = op("scalar",
                                            lambda sc, c=c, h2=h2: sc.activation(
                                                out=kp[c % 4][:], in_=dash2[:, h2, c, :], func=AF.Exp,
                                                bias=biasb[:, h2, c:c + 1]),
                                            wait=ew, inc=1)
                            def ctxf(t, c=c, hd=hd):
                                t.matmul(bank[2][0:128, 0:65], kp[c % 4][:, 0:128], vre[:, c, hd, :],
                                         start=(c == 0), stop=False)
                                t.matmul(bank[3][0:128, 0:65], kp[c % 4][:, 128:256], vre[:, c, hd, :],
                                         start=(c == 0), stop=False)
                                return t.matmul(bank[5][0:10, 0:65], kp[c % 4][:, 256:266], vre[:, c, hd, :],
                                                start=(c == 0), stop=False)
                            ctx_tok[c] = mmop([2, 3, 5] if c == 0 else [], ctxf,
                                              wait=[exp_tok[c], v_done])
                            prev_ctx[c % 4] = ctx_tok[c]
                        prev_exp = exp_tok[NCH - 1]
                        def ctxeps(t, hd=hd):
                            t.matmul(bank[2][0:128, 0:65], onesb_r[0:1, 0:128],
                                     epsv[0:1, hd * 65:(hd + 1) * 65], start=False, stop=True)
                            t.matmul(bank[3][0:128, 0:65], onesb_r[0:1, 0:128],
                                     epsv[0:1, hd * 65:(hd + 1) * 65], start=False, stop=True)
                            return t.matmul(bank[5][0:10, 0:65], onesb_r[0:1, 0:10],
                                            epsv[0:1, hd * 65:(hd + 1) * 65], start=False, stop=True)
                        ctx_fin = mmop([], ctxeps, wait=[ctx_tok[NCH - 1], epsv_tok, t_c7])
                        def ctxev_f(sc, h2=h2):
                            sc.activation(out=ctx_all[:, h2, 0, :], in_=bank[2][:, 0:65], func=AF.Copy)
                            sc.activation(out=ctx_all[:, h2, 1, :], in_=bank[3][:, 0:65], func=AF.Copy)
                            return sc.activation(out=ctx_all[0:10, h2, 2, :], in_=bank[5][0:10, 0:65], func=AF.Copy)
                        ctxev_toks[h2] = rdop("scalar", [2, 3, 5], ctxev_f, wait=[ctx_fin])
                        def csxf(t, h2=h2):
                            for mi, (m0, m1) in enumerate(MCH):
                                mm = t.matmul(bank[6][0:1, 200:265], ones_cb[0:m1 - m0, 0:1],
                                              ctx_all[0:m1 - m0, h2, mi, :], start=(mi == 0), stop=(mi == 2))
                            return mm
                        csx_mm = mmop([6], csxf, wait=[ctxev_toks[h2], t_c2])
                        csx_toks[h2] = rdop("scalar", [6],
                                            lambda sc, h2=h2: sc.activation(out=csx[0:1, h2, :],
                                                                            in_=bank[6][0:1, 200:265],
                                                                            func=AF.Copy, scale=RATIO * EPS),
                                            wait=[csx_mm])

                    for h2 in range(2):
                        b0p = 64 * h2
                        hd = 2 * hp + h2
                        t_q2 = op("scalar",
                                  lambda sc, b0p=b0p: sc.activation(out=ksq[:], in_=qT2[b0p:b0p + 64, :],
                                                                    func=AF.Square),
                                  wait=[q_ev[-1], prev_dgm], inc=1)
                        def dgqf(t):
                            for c in range(NCH):
                                mm = t.matmul(bank[7][:, c:c + 1], ksq[:, TS(c, 128)], half64[:],
                                              start=True, stop=True)
                            return mm
                        dgq = mmop([7], dgqf, wait=[t_q2, t_c3])
                        prev_dgm = dgq
                        dqcopy = rdop("vector", [7],
                                      lambda v, h2=h2: v.tensor_copy(out=diagB_sb[:, h2, :], in_=bank[7][:, 0:NCH]),
                                      wait=[dgq])
                        stq_tok = [None] * NCH
                        crow_tok = [None] * NCH
                        for j in range(4):
                            for ci in range(4):
                                c = 4 * j + ci
                                dq = mmop([c % 2],
                                          lambda t, c=c, b0p=b0p: t.matmul(
                                              bank[c % 2][:, 0:M], qT2[b0p:b0p + 64, TS(c, 128)],
                                              pj_sb[b0p:b0p + 64, :], start=True, stop=True),
                                          wait=[q_ev[c // 4], t_pj])
                                stq_tok[c] = rdop("vector", [c % 2],
                                                  lambda v, c=c, h2=h2: v.reduce_max(
                                                      scr12[:, h2:h2 + 1], bank[c % 2][:, 0:M], axis=AX.X),
                                                  wait=[dq])
                                ct = op("vector",
                                        lambda v, c=c, ci=ci, h2=h2: v.scalar_tensor_tensor(
                                            ctokb[:, h2, ci:ci + 1], diagB_sb[:, h2, c:c + 1], -LNRATIO,
                                            scr12[:, h2:h2 + 1], ALU.add, ALU.add),
                                        wait=[stq_tok[c], dqcopy], inc=1)
                                tpq = mmop([c % 2],
                                           lambda t, ci=ci, h2=h2: t.transpose(
                                               bank[c % 2][0:1, 300:428], ctokb[:, h2, ci:ci + 1], id_f[:]),
                                           wait=[ct])
                                crow_tok[c] = rdop("vector", [c % 2],
                                                   lambda v, c=c, ci=ci, h2=h2: v.tensor_copy(
                                                       out=crow2[0:1, h2, TS(ci, 128)], in_=bank[c % 2][0:1, 300:428]),
                                                   wait=[tpq])
                            af_tok = [None] * 3
                            for mi, (m0, m1) in enumerate(MCH):
                                bk = 2 + (mi % 2)
                                def dTf(t, mi=mi, m0=m0, m1=m1, j=j, b0p=b0p, h2=h2, bk=bk):
                                    t.matmul(bank[bk][0:m1 - m0, :], pj_sb[b0p:b0p + 64, m0:m1],
                                             qT2[b0p:b0p + 64, TS(j, 512)], start=True, stop=False)
                                    return t.matmul(bank[bk][0:m1 - m0, :], negf[0:1, 0:m1 - m0],
                                                    crow2[0:1, h2, :], start=False, stop=True)
                                dT = mmop([bk], dTf, wait=[q_ev[j], crow_tok[4 * j + 3], t_c5])
                                ex = rdop("scalar", [bk],
                                          lambda sc, m0=m0, m1=m1, bk=bk: sc.activation(
                                              out=qpe[0:m1 - m0, :], in_=bank[bk][0:m1 - m0, :], func=AF.Exp),
                                          wait=[dT, af_tok[mi - 1] if mi >= 1 else prev_aff])
                                af_tok[mi] = op("vector",
                                                lambda v, mi=mi, m0=m0, m1=m1, h2=h2: v.tensor_scalar_add(
                                                    qpt2[0:m1 - m0, h2, mi, :], qpe[0:m1 - m0, :], RATIO * EPS),
                                                wait=[ex, prev_at[h2]], inc=1)
                            prev_aff = af_tok[2]
                            def attf(t, h2=h2):
                                for mi, (m0, m1) in enumerate(MCH):
                                    mm = t.matmul(bank[4][0:65, :], ctx_all[0:m1 - m0, h2, mi, :],
                                                  qpt2[0:m1 - m0, h2, mi, :], start=(mi == 0), stop=(mi == 2))
                                return mm
                            at = mmop([4], attf, wait=[af_tok[0], af_tok[1], af_tok[2], ctxev_toks[h2]])
                            prev_at[h2] = at
                            rec = rdop("vector", [4],
                                       lambda v, h2=h2: v.reciprocal(dinv2[0:1, h2, :], bank[4][64:65, :]),
                                       wait=[at])
                            db = mmop([5],
                                      lambda t, h2=h2: t.matmul(bank[5][0:64, :], ones_rf[0:1, 0:64],
                                                                dinv2[0:1, h2, :], start=True, stop=True),
                                      wait=[rec, t_c1])
                            ae = rdop("scalar", [4],
                                      lambda sc, h2=h2: sc.activation(out=attf2[:, h2, :], in_=bank[4][0:64, :],
                                                                      func=AF.Copy),
                                      wait=[at, rec, prev_amul[h2]])
                            prev_ae[h2] = ae
                            amul = rdop("vector", [5],
                                        lambda v, j=j, hd=hd, b0p=b0p, h2=h2: v.tensor_mul(
                                            attnT[b0p:b0p + 64, hd // 2, TS(j, 512)], attf2[:, h2, :],
                                            bank[5][0:64, :]),
                                        wait=[ae, db])
                            prev_amul[h2] = amul
                            seq_term = amul

                P.barrier()
                inner1.close()

                t_wo = op("gpsimd", lambda g: g.dma_start(out=wvo_sb[:], in_=Wo.rearrange("(c p) f -> p c f", p=128)),
                          dma=True)
                innerC = ExitStack()
                _c = lambda nm, sh, dt: innerC.enter_context(nc.sbuf_tensor(f"{nm}_{s}", sh, dt))
                y_sb = _c("y_sb", [128, 8, 512], F32)
                xres = _c("xres", [128, 8, 512], F32)
                ybf = _c("ybf", [128, 8, 512], BF16)
                ysq = _c("ysq", [128, 8, 512], BF16)
                y1bt = _c("y1bt", [128, 8, 512], BF16)
                scr512 = _c("scr512", [128, 512], F32)
                rows = _c("rows", [1, 8, 512], F32)
                if True:
                    prev_apply = None
                    prev_dma = [None, None]
                    prev_y1c = None
                    for j in range(4):
                        t_xr = op("sync",
                                  lambda s_, s=s, j=j: s_.dma_start(
                                      out=xres[:], in_=xT[s].rearrange("(c p) t -> p c t", p=128)[:, :, TS(j, 512)]),
                                  wait=[prev_apply], dma=True)
                        ev_toks = []
                        for do_ in range(8):
                            pb = bank[do_ % 3]
                            def womm(t, do_=do_, j=j):
                                for dc in range(8):
                                    mm = t.matmul(pb[:] if False else bank[do_ % 3][:],
                                                  wvo_sb[:, dc, TS(do_, 128)], attnT[:, dc, TS(j, 512)],
                                                  start=(dc == 0), stop=(dc == 7))
                                return mm
                            ww = [t_wo] if (j == 0 and do_ == 0) else []
                            if do_ >= 3:
                                ww.append(ev_toks[do_ - 3])
                            elif j > 0:
                                ww.append(prev_apply)
                            wtok = op("tensor", womm, wait=ww, inc=1)
                            ew_ = [wtok, t_bo, t_xr]
                            if do_ == 0 and j > 0:
                                ew_ += [prev_dma[0], prev_y1c]
                            ev_toks.append(op("vector",
                                             lambda v, do_=do_: v.scalar_tensor_tensor(
                                                 y_sb[:, do_, :], bank[do_ % 3][:], bo_sb[:, do_:do_ + 1],
                                                 xres[:, do_, :], ALU.add, ALU.add),
                                             wait=ew_, inc=1))
                        t_add = ev_toks[-1]
                        t_ybf = op("vector", lambda v: v.tensor_copy(out=ybf[:], in_=y_sb[:]),
                                   wait=[t_add], inc=1)
                        t_ysq = op("scalar", lambda sc: sc.activation(out=ysq[:], in_=y_sb[:], func=AF.Square),
                                   wait=[t_add], inc=1)
                        def summ(t):
                            for dc in range(8):
                                mm = t.matmul(bank[3][0:1, :], ones_cb[:], ybf[:, dc, :],
                                              start=(dc == 0), stop=(dc == 7))
                            return mm
                        def sqmm(t):
                            for dc in range(8):
                                mm = t.matmul(bank[4][0:1, :], ones_cb[:], ysq[:, dc, :],
                                              start=(dc == 0), stop=(dc == 7))
                            return mm
                        sm_tok = op("tensor", summ, wait=[t_ybf, t_c2, prev_dma[0]], inc=1)
                        sq_tok = op("tensor", sqmm, wait=[t_ysq], inc=1)
                        mean_tok = op("vector",
                                      lambda v: v.tensor_scalar_mul(rows[0:1, 0, :], bank[3][0:1, :], 1.0 / D),
                                      wait=[sm_tok], inc=1)
                        var_tok = op("vector",
                                     lambda v: v.tensor_scalar_mul(rows[0:1, 1, :], bank[4][0:1, :], 1.0 / D),
                                     wait=[sq_tok], inc=1)
                        msq_tok = op("vector",
                                     lambda v: v.tensor_mul(rows[0:1, 3, :], rows[0:1, 0, :], rows[0:1, 0, :]),
                                     wait=[mean_tok], inc=1)
                        var2_tok = op("vector",
                                      lambda v: v.tensor_sub(rows[0:1, 1, :], rows[0:1, 1, :], rows[0:1, 3, :]),
                                      wait=[var_tok, msq_tok], inc=1)
                        ln_tok = op("scalar",
                                    lambda sc: sc.activation(out=rows[0:1, 3, :], in_=rows[0:1, 1, :],
                                                             func=AF.Ln, bias=eps_sb[:]),
                                    wait=[var2_tok, t_c4], inc=1)
                        rstd_tok = op("scalar",
                                      lambda sc: sc.activation(out=rows[0:1, 2, :], in_=rows[0:1, 3, :],
                                                               func=AF.Exp, scale=-0.5),
                                      wait=[ln_tok], inc=1)
                        mb_tok = op("tensor",
                                    lambda t: t.matmul(bank[5][:], ones_rf[:], rows[0:1, 0, :], start=True, stop=True),
                                    wait=[var2_tok, prev_apply], inc=1)
                        rb_tok = op("tensor",
                                    lambda t: t.matmul(bank[6][:], ones_rf[:], rows[0:1, 2, :], start=True, stop=True),
                                    wait=[rstd_tok, prev_apply], inc=1)
                        ap_toks = []
                        for dc in range(8):
                            t1 = op("vector",
                                    lambda v, dc=dc: v.tensor_sub(scr512[:], y_sb[:, dc, :], bank[5][:]),
                                    wait=[mb_tok] + ([] if dc else [rb_tok]), inc=1)
                            t2 = op("vector",
                                    lambda v, dc=dc: v.tensor_mul(y_sb[:, dc, :], scr512[:], bank[6][:]),
                                    wait=[t1], inc=1)
                            ap_toks.append(t2)
                        t_y1c = op("vector", lambda v: v.tensor_copy(out=y1bt[:], in_=y_sb[:]),
                                   wait=[ap_toks[-1], prev_dma[1]], inc=1)
                        gt0 = s * N + j * 512
                        d1 = op("sync",
                                lambda s_, gt0=gt0: s_.dma_start(
                                    out=y1f.rearrange("(c p) t -> p c t", p=128)[:, :, gt0:gt0 + 512], in_=y_sb[:]),
                                wait=[ap_toks[-1]], dma=True)
                        d2 = op("gpsimd",
                                lambda g, gt0=gt0: g.dma_start(
                                    out=y1bd.rearrange("(c p) t -> p c t", p=128)[:, :, gt0:gt0 + 512], in_=y1bt[:]),
                                wait=[t_y1c], dma=True)
                        prev_apply = ap_toks[-1]
                        prev_y1c = t_y1c
                        prev_dma = [d1, d2]
                    P.barrier()
                innerC.close()
                prev_seq_term = None
                innerC.close()
                prev_seq_term = None
            P.barrier()

        NT8 = NTOK // 512
        innerB = ExitStack()
        wTb = innerB.enter_context(nc.sbuf_tensor("wTb", [E, NTOK], BF16))
        with nc.sbuf_tensor("y1b", [128, 8, NTOK], BF16) as y1b:
            t_y1b = op("gpsimd",
                       lambda g: g.dma_start(out=y1b[:], in_=y1bd.rearrange("(c p) t -> p c t", p=128)),
                       dma=True)

            innerR = ExitStack()
            _r = lambda nm, sh, dt: innerR.enter_context(nc.sbuf_tensor(nm, sh, dt))
            lgT = _r("lgT", [8, 512], F32)
            lgtok = _r("lgtok", [128, 8], F32)
            m8 = _r("m8", [128, 8], F32)
            nv1 = _r("nv1", [128, 1], F32)
            maskb = _r("maskb", [128, 8], F32)
            expv = _r("expv", [128, 8], F32)
            mev = _r("mev", [128, 8], F32)
            zz = _r("zz", [128, 1], F32)
            zi = _r("zi", [128, 1], F32)
            wtok = _r("wtok", [128, 8], F32)
            if True:
                prev_wt = None
                ex_prev = None
                for jt in range(NT8):
                    def lgmm(t, jt=jt):
                        for dc in range(8):
                            mm = t.matmul(bank[0][0:8, :], wr_sb[:, dc, :], y1b[:, dc, TS(jt, 512)],
                                          start=(dc == 0), stop=(dc == 7))
                        return mm
                    lg_tok = op("tensor", lgmm, wait=[t_y1b, t_wr, prev_wt], inc=1)
                    lge_tok = op("vector", lambda v: v.tensor_copy(out=lgT[:], in_=bank[0][0:8, :]),
                                 wait=[lg_tok], inc=1)
                    for ci in range(4):
                        c = 4 * jt + ci
                        tp_tok = op("tensor",
                                    lambda t, ci=ci: t.matmul(bank[1][:, 0:8], lgT[0:8, TS(ci, 128)],
                                                              id_f[0:8, 0:8], start=True, stop=True),
                                    wait=[lge_tok] + ([prev_wt] if ci == 0 else []), inc=1)
                        ba_tok = op("vector",
                                    lambda v: v.tensor_add(lgtok[:], bank[1][:, 0:8], br_sb[:]),
                                    wait=[tp_tok, t_br, ex_prev], inc=1)
                        m8_tok = op("vector", lambda v: v.max(out=m8[:], in_=lgtok[:]), wait=[ba_tok], inc=1)
                        nv_tok = op("vector", lambda v: v.tensor_scalar_mul(nv1[:], m8[:, 0:1], -1.0),
                                    wait=[m8_tok, ex_prev], inc=1)
                        mk_tok = op("vector",
                                    lambda v: v.tensor_scalar(maskb[:], lgtok[:], m8[:, 1:2], None, ALU.is_ge),
                                    wait=[m8_tok], inc=1)
                        ex_tok = op("scalar",
                                    lambda sc: sc.activation(out=expv[:], in_=lgtok[:], func=AF.Exp, bias=nv1[:]),
                                    wait=[nv_tok], inc=1)
                        ex_prev = ex_tok
                        me_tok = op("vector", lambda v: v.tensor_mul(mev[:], maskb[:], expv[:]),
                                    wait=[mk_tok, ex_tok], inc=1)
                        z_tok = op("vector", lambda v: v.reduce_sum(zz[:], mev[:], axis=AX.X),
                                   wait=[me_tok], inc=1)
                        zi_tok = op("vector", lambda v: v.reciprocal(zi[:], zz[:]), wait=[z_tok], inc=1)
                        wt_tok = op("vector", lambda v: v.tensor_scalar_mul(wtok[:], mev[:], zi[:]),
                                    wait=[zi_tok], inc=1)
                        wtp_tok = op("tensor",
                                     lambda t: t.matmul(bank[2][0:8, 0:128], wtok[:], id_f[:], start=True, stop=True),
                                     wait=[wt_tok] + ([prev_wt] if ci == 0 else []), inc=1)
                        prev_wt = op("vector",
                                     lambda v, c=c: v.tensor_copy(out=wTb[0:8, TS(c, 128)], in_=bank[2][0:8, 0:128]),
                                     wait=[wtp_tok], inc=1)
            innerR.close()
            P.barrier()

            NCB = 4
            innerW1 = ExitStack()
            _w = lambda nm, sh, dt: innerW1.enter_context(nc.sbuf_tensor(nm, sh, dt))
            w1a = _w("w1a", [128, 2, 8, 512], BF16)
            w1g = _w("w1g", [128, 2, 8, 512], BF16)
            wbc = _w("wbc", [128, 8, 512], F32)
            asl = _w("asl", [128, 512], BF16)
            gw = _w("gw", [128, 512], BF16)
            actout = _w("actout", [128, 2, 4, 512], BF16)
            if True:
                w1_last_use = {}
                act_dma = {}
                prev_gw = None
                prev_asl = None
                prev_act = {}
                wbc_prev = None
                for e in range(E):
                    wb_toks = []
                    for tt in range(NT8):
                        wbm = op("tensor",
                                 lambda t, e=e, tt=tt: t.matmul(bank[4][:], oh_sb[0:8, TS(e, 128)],
                                                                wTb[0:8, TS(tt, 512)], start=True, stop=True),
                                 wait=[t_oh, prev_wt, wbc_prev] + ([wb_toks[-1]] if wb_toks else []), inc=1)
                        wb_toks.append(op("vector",
                                          lambda v, tt=tt: v.tensor_copy(out=wbc[:, tt, :], in_=bank[4][:]),
                                          wait=[wbm], inc=1))
                    wbc_done = wb_toks[-1]
                    wbc_prev = wbc_done
                    for cb in range(NCB):
                        gcb = e * NCB + cb
                        par = gcb % 2
                        dw = [w1_last_use.get(par)]
                        ta = op("gpsimd",
                                lambda g, e=e, cb=cb, par=par: g.dma_start(
                                    out=w1a[:, par], in_=W1[e][:, TS(cb, 512)].rearrange("(c p) f -> p c f", p=128)),
                                wait=dw, dma=True)
                        tg = op("gpsimd",
                                lambda g, e=e, cb=cb, par=par: g.dma_start(
                                    out=w1g[:, par], in_=W1[e][:, H + cb * 512:H + cb * 512 + 512]
                                    .rearrange("(c p) f -> p c f", p=128)),
                                dma=True)
                        for tt in range(NT8):
                            for jj in range(4):
                                i = tt * 4 + jj
                                ab = bank[0 + (i % 2)]
                                gb = bank[2 + (i % 2)]
                                jjg = cb * 4 + jj
                                def amm(t, par=par, jj=jj, tt=tt, ab=ab):
                                    for dc in range(8):
                                        mm = t.matmul(ab[:], w1a[:, par, dc, TS(jj, 128)],
                                                      y1b[:, dc, TS(tt, 512)], start=(dc == 0), stop=(dc == 7))
                                    return mm
                                aw = [ta if (tt == 0 and jj == 0) else None,
                                      prev_asl and prev_asl.get(i % 2)]
                                a_tok = op("tensor", amm, wait=[w for w in aw if w], inc=1)
                                def gmm(t, par=par, jj=jj, tt=tt, gb=gb):
                                    for dc in range(8):
                                        mm = t.matmul(gb[:], w1g[:, par, dc, TS(jj, 128)],
                                                      y1b[:, dc, TS(tt, 512)], start=(dc == 0), stop=(dc == 7))
                                    return mm
                                gw_ = [tg if (tt == 0 and jj == 0) else None,
                                       prev_gw and prev_gw.get(i % 2)]
                                g_tok = op("tensor", gmm, wait=[w for w in gw_ if w], inc=1)
                                if tt == NT8 - 1 and jj == 3:
                                    w1_last_use[par] = g_tok
                                sl_tok = op("scalar",
                                            lambda sc, e=e, jjg=jjg, ab=ab: sc.activation(
                                                out=asl[:], in_=ab[:], func=AF.Silu,
                                                bias=b1_sb[:, e, jjg:jjg + 1]),
                                            wait=[a_tok, t_b1, prev_act.get("asl")], inc=1)
                                gw_tok = op("vector",
                                            lambda v, e=e, jjg=jjg, tt=tt, gb=gb: v.scalar_tensor_tensor(
                                                gw[:], gb[:], b1_sb[:, e, 16 + jjg:16 + jjg + 1],
                                                wbc[:, tt, :], ALU.add, ALU.mult),
                                            wait=[g_tok, wbc_done, prev_act.get("gw")], inc=1)
                                ac_tok = op("vector",
                                            lambda v, jj=jj, tt=tt: v.tensor_mul(
                                                actout[:, tt % 2, jj, :], asl[:], gw[:]),
                                            wait=[sl_tok, gw_tok, act_dma.get(tt % 2)], inc=1)
                                prev_asl = prev_asl or {}
                                prev_gw = prev_gw or {}
                                prev_asl[i % 2] = sl_tok
                                prev_gw[i % 2] = gw_tok
                                prev_act["asl"] = ac_tok
                                prev_act["gw"] = ac_tok
                            act_dma[tt % 2] = op("sync",
                                                 lambda s_, e=e, cb=cb, tt=tt: s_.dma_start(
                                                     out=actd[e].rearrange("(c p) t -> p c t", p=128)
                                                     [:, TS(cb, 4), TS(tt, 512)],
                                                     in_=actout[:, tt % 2]),
                                                 wait=[prev_act["asl"]], dma=True)
                act_w1_done = act_dma[(NT8 - 1) % 2]
            innerW1.close()
            P.barrier()

        innerW2 = ExitStack()
        _w2 = lambda nm, sh, dt: innerW2.enter_context(nc.sbuf_tensor(nm, sh, dt))
        acc = _w2("acc", [128, 8, N], F32)
        w2b = _w2("w2b", [128, 2, 16, D], BF16)
        actin = _w2("actin", [128, 2, 16, 512], BF16)
        ybf2 = _w2("ybf2", [128, 8, 512], BF16)
        ysq2 = _w2("ysq2", [128, 8, 512], BF16)
        oscr = _w2("oscr", [128, 512], F32)
        rows2 = _w2("rows2", [1, 4, 512], F32)
        if True:
            for half in range(2):
                t_acc = op("sync",
                           lambda s_, half=half: s_.dma_start(
                               out=acc[:], in_=y1f.rearrange("(c p) t -> p c t", p=128)[:, :, TS(half, N)]),
                           wait=[], dma=True)
                w2_last = {}
                act_last = {}
                add_toks = {}
                for e in range(E):
                    par = e % 2
                    tw2 = op("gpsimd",
                             lambda g, e=e, par=par: g.dma_start(
                                 out=w2b[:, par], in_=W2[e].rearrange("(c p) f -> p c f", p=128)),
                             wait=[w2_last.get(par)], dma=True)
                    for tt in range(4):
                        gt = half * 4 + tt
                        apar = tt % 2
                        tact = op("gpsimd",
                                  lambda g, e=e, gt=gt, apar=apar: g.dma_start(
                                      out=actin[:, apar], in_=actd[e].rearrange("(c p) t -> p c t", p=128)
                                      [:, :, TS(gt, 512)]),
                                  wait=[act_last.get(apar), act_w1_done if (e == 0 and tt <= 1) else None],
                                  dma=True)
                        for dc in range(8):
                            i = tt * 8 + dc
                            ob = bank[i % 4]
                            def w2mm(t, e=e, gt=gt, dc=dc, apar=apar, par=par, ob=ob):
                                mm = t.matmul(ob[:], b2_sb[0:8, TS(dc, 128)], wTb[0:8, TS(gt, 512)],
                                              start=True, stop=False)
                                for h in range(16):
                                    mm = t.matmul(ob[:], w2b[:, par, h, TS(dc, 128)], actin[:, apar, h, :],
                                                  start=False, stop=(h == 15))
                                return mm
                            mm_tok = op("tensor", w2mm,
                                        wait=[tw2 if (tt == 0 and dc == 0) else None,
                                              tact if dc == 0 else None,
                                              t_b2 if (half == 0 and e == 0 and tt == 0 and dc == 0) else None,
                                              t_acc if (e == 0 and dc == 0) else None,
                                              add_toks.get(i % 4)], inc=1)
                            add_toks[i % 4] = op("vector",
                                                 lambda v, tt=tt, dc=dc, ob=ob: v.tensor_add(
                                                     acc[:, dc, TS(tt, 512)], acc[:, dc, TS(tt, 512)], ob[:]),
                                                 wait=[mm_tok, t_acc if e == 0 else None], inc=1)
                            if dc == 7:
                                act_last[apar] = add_toks[i % 4]
                            if tt == 3 and dc == 7:
                                w2_last[par] = add_toks[i % 4]
                lastadd = [add_toks[i] for i in range(4)]
                prev_out = None
                for tt in range(4):
                    t_ybf = op("vector",
                               lambda v, tt=tt: v.tensor_copy(out=ybf2[:], in_=acc[:].rearrange(
                                   "p c (a t) -> p c a t", t=512)[:, :, tt, :]),
                               wait=lastadd + [prev_out], inc=1)
                    t_ysq = op("scalar",
                               lambda sc, tt=tt: sc.activation(out=ysq2[:], in_=acc[:].rearrange(
                                   "p c (a t) -> p c a t", t=512)[:, :, tt, :], func=AF.Square),
                               wait=lastadd + [prev_out], inc=1)
                    def summ2(t):
                        for dc in range(8):
                            mm = t.matmul(bank[4][0:1, :], ones_cb[:], ybf2[:, dc, :],
                                          start=(dc == 0), stop=(dc == 7))
                        return mm
                    def sqmm2(t):
                        for dc in range(8):
                            mm = t.matmul(bank[5][0:1, :], ones_cb[:], ysq2[:, dc, :],
                                          start=(dc == 0), stop=(dc == 7))
                        return mm
                    sm_tok = op("tensor", summ2, wait=[t_ybf, prev_out], inc=1)
                    sq_tok = op("tensor", sqmm2, wait=[t_ysq], inc=1)
                    mean_tok = op("vector",
                                  lambda v: v.tensor_scalar_mul(rows2[0:1, 0, :], bank[4][0:1, :], 1.0 / D),
                                  wait=[sm_tok], inc=1)
                    sq2_tok = op("vector",
                                 lambda v: v.tensor_scalar_mul(rows2[0:1, 1, :], bank[5][0:1, :], 1.0 / D),
                                 wait=[sq_tok], inc=1)
                    msq_tok = op("vector",
                                 lambda v: v.tensor_mul(rows2[0:1, 3, :], rows2[0:1, 0, :], rows2[0:1, 0, :]),
                                 wait=[mean_tok], inc=1)
                    var_tok = op("vector",
                                 lambda v: v.tensor_sub(rows2[0:1, 1, :], rows2[0:1, 1, :], rows2[0:1, 3, :]),
                                 wait=[sq2_tok, msq_tok], inc=1)
                    ln_tok = op("scalar",
                                lambda sc: sc.activation(out=rows2[0:1, 3, :], in_=rows2[0:1, 1, :],
                                                         func=AF.Ln, bias=eps_sb[:]),
                                wait=[var_tok, t_c4], inc=1)
                    rstd_tok = op("scalar",
                                  lambda sc: sc.activation(out=rows2[0:1, 2, :], in_=rows2[0:1, 3, :],
                                                           func=AF.Exp, scale=-0.5),
                                  wait=[ln_tok], inc=1)
                    mb_tok = op("tensor",
                                lambda t: t.matmul(bank[6][:], ones_rf[:], rows2[0:1, 0, :], start=True, stop=True),
                                wait=[var_tok, prev_out], inc=1)
                    rb_tok = op("tensor",
                                lambda t: t.matmul(bank[7][:], ones_rf[:], rows2[0:1, 2, :], start=True, stop=True),
                                wait=[rstd_tok, prev_out], inc=1)
                    out_toks = []
                    for dc in range(8):
                        t1 = op("vector",
                                lambda v, tt=tt, dc=dc: v.tensor_sub(oscr[:], acc[:, dc, TS(tt, 512)], bank[6][:]),
                                wait=[mb_tok] + ([rb_tok] if dc == 0 else []), inc=1)
                        t2 = op("vector",
                                lambda v, tt=tt, dc=dc: v.tensor_mul(acc[:, dc, TS(tt, 512)], oscr[:], bank[7][:]),
                                wait=[t1], inc=1)
                        out_toks.append(t2)
                    dout = op("sync",
                              lambda s_, half=half, tt=tt: s_.dma_start(
                                  out=outT[half].rearrange("(c p) t -> p c t", p=128)[:, :, TS(tt, 512)],
                                  in_=acc[:].rearrange("p c (a t) -> p c a t", t=512)[:, :, tt, :]),
                              wait=[out_toks[-1]], dma=True)
                    prev_out = dout
                P.barrier()
        innerW2.close()
        innerB.close()

        fin1 = P.cnt["sync"]
        op("sync", lambda s_: s_.nop(), wait=[("sync", fin1)])
        fin2 = P.cnt["gpsimd"]
        op("gpsimd", lambda g: g.nop(), wait=[("gpsimd", fin2)])

        with nc.Block() as block:
            P.emit(block)
    return nc


_NC_CACHE = {}


def _get_nc():
    if "nc" not in _NC_CACHE:
        _NC_CACHE["nc"] = build_program()
    return _NC_CACHE["nc"]


def kernel(L0_attn_x, L0_attn_Wq, L0_attn_Wk, L0_attn_Wv, L0_attn_Wo, L0_attn_bo,
           L0_attn_proj, L0_ffn_W1, L0_ffn_b1, L0_ffn_W2, L0_ffn_b2, L0_ffn_Wr,
           L0_ffn_br):
    f32 = np.float32
    x = np.asarray(L0_attn_x, f32)
    xTt = np.ascontiguousarray(x.transpose(0, 2, 1))
    projT = np.ascontiguousarray(np.asarray(L0_attn_proj, f32).T)
    bo_l = np.ascontiguousarray(np.asarray(L0_attn_bo, f32).reshape(8, 128).T)
    b1_l = np.ascontiguousarray(np.asarray(L0_ffn_b1, f32).reshape(E, 32, 128).transpose(2, 0, 1))
    br_b = np.ascontiguousarray(np.broadcast_to(np.asarray(L0_ffn_br, f32), (128, E)))
    idf = np.eye(128, dtype=f32)
    oh8 = np.zeros((E, E * 128), dtype=ml_dtypes.bfloat16)
    for e in range(E):
        oh8[e, e * 128:(e + 1) * 128] = 1
    common = {
        "Wq": np.ascontiguousarray(np.asarray(L0_attn_Wq, f32)),
        "Wk": np.ascontiguousarray(np.asarray(L0_attn_Wk, f32)),
        "Wv": np.ascontiguousarray(np.asarray(L0_attn_Wv, f32)),
        "Wo": np.ascontiguousarray(np.asarray(L0_attn_Wo, f32)),
        "projT": projT, "bo_l": bo_l,
        "W1": np.ascontiguousarray(np.asarray(L0_ffn_W1, f32)),
        "b1_l": b1_l,
        "W2": np.ascontiguousarray(np.asarray(L0_ffn_W2, f32)),
        "b2": np.ascontiguousarray(np.asarray(L0_ffn_b2, f32)),
        "Wr": np.ascontiguousarray(np.asarray(L0_ffn_Wr, f32)),
        "br_b": br_b, "idf": idf, "oh8": oh8,
    }
    in_maps = []
    for i in range(8):
        m = dict(common)
        m["xT"] = np.ascontiguousarray(xTt[2 * i:2 * i + 2])
        in_maps.append(m)
    nc = _get_nc()
    res = run_bass_kernel_spmd(nc, in_maps, core_ids=list(range(8)))
    shards = [res.results[i]["outT"] for i in range(8)]
    out = np.concatenate(shards, axis=0)
    return np.ascontiguousarray(out.transpose(0, 2, 1)).astype(np.float32)


# version 07
# speedup vs baseline: 1.1939x; 1.1939x over previous
import numpy as np
import ml_dtypes
from contextlib import ExitStack

import concourse.bass as bass
import concourse.mybir as mybir
from concourse.bass_utils import run_bass_kernel_spmd

F32 = mybir.dt.float32
BF16 = mybir.dt.bfloat16
AF = mybir.ActivationFunctionType
ALU = mybir.AluOpType
AX = mybir.AxisListType
TS = bass.ts

D = 1024
NSEQ = 2
N = 2048
NTOK = NSEQ * N
NCH = N // 128
D_HEAD = 64
NHEADS = 16
M = 266
EPS = 1e-4
RATIO = M ** -0.5
DN = D_HEAD ** -0.25
LNRATIO = float(np.log(RATIO))
MCH = [(0, 128), (128, 256), (256, 266)]
H = 2048
E = 8
LN_EPS = 1e-5


class Prog:
    ENGS = ("sync", "gpsimd", "tensor", "vector", "scalar")

    def __init__(self, nc, stack):
        self.nc = nc
        self.ops = {e: [] for e in self.ENGS}
        self.sems = {e: stack.enter_context(nc.semaphore(f"sem_{e}")) for e in self.ENGS}
        self.cnt = {e: 0 for e in self.ENGS}
        self.pending = None

    def op(self, eng, fn, wait=(), inc=0, dma=False):
        if dma:
            inc = 16
        waits = [w for w in wait if w is not None]
        if self.pending is not None:
            waits = self.pending + waits
            if not hasattr(self, "_seen"):
                self._seen = set()
            if eng not in self._seen:
                self._seen.add(eng)
                if self._seen == set(self.ENGS):
                    self.pending = None
                    del self._seen
        tok = None
        if inc:
            self.cnt[eng] += inc
            tok = (eng, self.cnt[eng])
        self.ops[eng].append((tuple(waits), fn, inc))
        return tok

    def barrier(self):
        toks = [(e, self.cnt[e]) for e in self.ENGS if self.cnt[e] > 0]
        if self.pending is not None:
            pass
        self.pending = toks
        if hasattr(self, "_seen"):
            del self._seen

    def emit(self, block):
        for e in self.ENGS:
            entries = self.ops[e]
            if not entries:
                continue
            sems = self.sems

            def body(engine, entries=entries, e=e):
                waited = {}
                for waits, fn, inc in entries:
                    for (we, wv) in waits:
                        if waited.get(we, 0) >= wv:
                            continue
                        engine.wait_ge(sems[we], wv)
                        waited[we] = wv
                    inst = fn(engine)
                    if inc:
                        inst.then_inc(sems[e], inc)
            getattr(block, e)(body)


def build_program():
    nc = bass.Bass("TRN2", target_bir_lowering=False)
    xT = nc.dram_tensor("xT", [NSEQ, D, N], F32, kind="ExternalInput").ap()
    Wq = nc.dram_tensor("Wq", [D, D], F32, kind="ExternalInput").ap()
    Wk = nc.dram_tensor("Wk", [D, D], F32, kind="ExternalInput").ap()
    Wv = nc.dram_tensor("Wv", [D, D], F32, kind="ExternalInput").ap()
    Wo = nc.dram_tensor("Wo", [D, D], F32, kind="ExternalInput").ap()
    projT = nc.dram_tensor("projT", [D_HEAD, M], F32, kind="ExternalInput").ap()
    bo_l = nc.dram_tensor("bo_l", [128, 8], F32, kind="ExternalInput").ap()
    W1 = nc.dram_tensor("W1", [E, D, 2 * H], F32, kind="ExternalInput").ap()
    b1_l = nc.dram_tensor("b1_l", [128, E, 32], F32, kind="ExternalInput").ap()
    W2 = nc.dram_tensor("W2", [E, H, D], F32, kind="ExternalInput").ap()
    b2 = nc.dram_tensor("b2", [E, D], F32, kind="ExternalInput").ap()
    Wr = nc.dram_tensor("Wr", [D, E], F32, kind="ExternalInput").ap()
    br_b = nc.dram_tensor("br_b", [128, E], F32, kind="ExternalInput").ap()
    idf = nc.dram_tensor("idf", [128, 128], F32, kind="ExternalInput").ap()
    oh8 = nc.dram_tensor("oh8", [E, E * 128], BF16, kind="ExternalInput").ap()
    outT = nc.dram_tensor("outT", [NSEQ, D, N], F32, kind="ExternalOutput").ap()
    y1f = nc.dram_tensor("y1f", [D, NTOK], F32, kind="Internal").ap()
    y1bd = nc.dram_tensor("y1bd", [D, NTOK], BF16, kind="Internal").ap()
    actd = nc.dram_tensor("actd", [E, H, NTOK], BF16, kind="Internal").ap()

    stack = ExitStack()
    sbt = lambda nm, sh, dt: stack.enter_context(nc.sbuf_tensor(nm, sh, dt))
    with stack:
        P = Prog(nc, stack)
        op = P.op

        id_f = sbt("id_f", [128, 128], F32)
        ones_rf = sbt("ones_rf", [1, 128], F32)
        ones_cb = sbt("ones_cb", [128, 1], BF16)
        half64 = sbt("half64", [64, 1], BF16)
        pj_sb = sbt("pj_sb", [128, M], BF16)
        bo_sb = sbt("bo_sb", [128, 8], F32)
        b1_sb = sbt("b1_sb", [128, E, 32], F32)
        br_sb = sbt("br_sb", [128, E], F32)
        b2_sb = sbt("b2_sb", [E, D], BF16)
        wr_sb = sbt("wr_sb", [128, 8, E], BF16)
        oh_sb = sbt("oh_sb", [E, E * 128], BF16)
        eps_sb = sbt("eps_sb", [1, 1], F32)
        negf = sbt("negf", [1, 128], F32)
        negb = sbt("negb", [1, 128], BF16)
        onesb_r = sbt("onesb_r", [1, 512], BF16)

        bank = [stack.enter_context(nc.psum_tensor(f"bank{i}", [128, 512], F32))
                for i in range(8)]

        t_id = op("sync", lambda s: s.dma_start(out=id_f[:], in_=idf), dma=True)
        t_bo = op("sync", lambda s: s.dma_start(out=bo_sb[:], in_=bo_l), dma=True)
        t_b1 = op("sync", lambda s: s.dma_start(out=b1_sb[:], in_=b1_l), dma=True)
        t_br = op("sync", lambda s: s.dma_start(out=br_sb[:], in_=br_b), dma=True)
        t_oh = op("sync", lambda s: s.dma_start(out=oh_sb[:], in_=oh8), dma=True)
        t_pj0 = op("gpsimd", lambda g: g.dma_start(out=pj_sb[0:64, :], in_=projT), dma=True)
        t_pj = op("gpsimd", lambda g: g.dma_start(out=pj_sb[64:128, :], in_=projT), dma=True)
        t_b2 = op("gpsimd", lambda g: g.dma_start(out=b2_sb[:], in_=b2), dma=True)
        t_wr = op("gpsimd", lambda g: g.dma_start(out=wr_sb[:], in_=Wr.rearrange("(c p) e -> p c e", p=128)), dma=True)
        t_c1 = op("vector", lambda v: v.memset(ones_rf[:], 1.0), inc=1)
        t_c2 = op("vector", lambda v: v.memset(ones_cb[:], 1.0), inc=1)
        t_c3 = op("vector", lambda v: v.memset(half64[:], 0.5), inc=1)
        t_c4 = op("vector", lambda v: v.memset(eps_sb[:], LN_EPS), inc=1)
        t_c5 = op("vector", lambda v: v.memset(negf[:], -1.0), inc=1)
        t_c6 = op("vector", lambda v: v.memset(negb[:], -1.0), inc=1)
        t_c7 = op("vector", lambda v: v.memset(onesb_r[:], 1.0), inc=1)

        with nc.sbuf_tensor("wk_sb", [128, 8, D], BF16) as wk_sb, \
             nc.sbuf_tensor("wq_sb", [128, 8, D], BF16) as wq_sb, \
             nc.sbuf_tensor("wvo_sb", [128, 8, D], BF16) as wvo_sb, \
             nc.sbuf_tensor("attnT", [128, 8, N], BF16) as attnT:

            t_wk = op("gpsimd", lambda g: g.dma_start(out=wk_sb[:], in_=Wk.rearrange("(c p) f -> p c f", p=128)), dma=True)
            t_wq = op("gpsimd", lambda g: g.dma_start(out=wq_sb[:], in_=Wq.rearrange("(c p) f -> p c f", p=128)), dma=True)

            prev_seq_term = None
            for s in range(NSEQ):
                t_wv = op("gpsimd", lambda g: g.dma_start(out=wvo_sb[:], in_=Wv.rearrange("(c p) f -> p c f", p=128)),
                          wait=[prev_seq_term], dma=True)

                inner1 = ExitStack()
                _i = lambda nm, sh, dt: inner1.enter_context(nc.sbuf_tensor(f"{nm}_{s}", sh, dt))
                xb = _i("xb", [128, 8, N], BF16)
                vaug = _i("vaug", [128, NCH, 16 * 65], BF16)
                kT2 = _i("kT2", [128, N], BF16)
                qT2 = _i("qT2", [128, N], BF16)
                ksq = _i("ksq", [64, N], BF16)
                dash2 = _i("dash2", [128, 2, NCH, M], BF16)
                diagA_sb = _i("diagA_sb", [128, 2, NCH], F32)
                diagB_sb = _i("diagB_sb", [128, 2, NCH], F32)
                stabacc = _i("stabacc", [128, 4], F32)
                scr4 = _i("scr4", [128, 2, 4], F32)
                nstab_r = _i("nstab_r", [1, 2], F32)
                nstab = _i("nstab", [1, 2], F32)
                nstab_b = _i("nstab_b", [128, 2], F32)
                biasb = _i("biasb", [128, 2, NCH], F32)
                kpb = [_i(f"kpb{i}", [128, M], BF16) for i in range(4)]
                ctx_all = _i("ctx_all", [128, 2, 3, 65], BF16)
                ctokb = _i("ctokb", [128, 2, 4], F32)
                crow2 = _i("crow2", [1, 2, 512], BF16)
                epsv = _i("epsv", [1, 16 * 65], BF16)
                lnrow = _i("lnrow", [1, 512], F32)
                csx = _i("csx", [1, 2, 65], BF16)
                qpe = _i("qpe", [128, 512], BF16)
                qpt2 = _i("qpt2", [128, 2, 3, 512], BF16)
                dinv2 = _i("dinv2", [1, 2, 2, 512], BF16)
                attf2 = _i("attf2", [64, 2, 2, 512], BF16)
                kp = kpb
                vre = vaug[:].rearrange("p c (h e) -> p c h e", e=65)

                t_xb = op("gpsimd",
                          lambda g, s=s: g.dma_start(out=xb[:], in_=xT[s].rearrange("(c p) t -> p c t", p=128)),
                          wait=[prev_seq_term], dma=True)
                t_vone = op("vector", lambda v: v.memset(vre[:, :, :, 64:65], 1.0),
                            wait=[prev_seq_term], inc=1)

                rd = {i: [] for i in range(8)}

                def mmop(banks, fn, wait=()):
                    w = list(wait)
                    for b in banks:
                        w += rd[b]
                        rd[b] = []
                    return op("tensor", fn, wait=w, inc=1)

                def rdop(eng, banks, fn, wait=(), inc=1):
                    tok = op(eng, fn, wait=list(wait), inc=inc)
                    for b in banks:
                        rd[b].append(tok)
                    return tok

                v_ev = {}
                for c in range(NCH):
                    for half in range(2):
                        bk = 2 + (c % 2) * 2 + half
                        def vmm(t, c=c, half=half, bk=bk):
                            for dc in range(8):
                                mm = t.matmul(bank[bk][:], xb[:, dc, TS(c, 128)], wvo_sb[:, dc, TS(half, 512)],
                                              start=(dc == 0), stop=(dc == 7))
                            return mm
                        tok_mm = mmop([bk], vmm, wait=[t_xb, t_wv])
                        def vev(sc, c=c, half=half, bk=bk):
                            dst = vre[:, c, TS(half, 8), 0:64]
                            src = bank[bk][:].rearrange("p (h e) -> p h e", e=64)
                            return sc.activation(out=dst, in_=src, func=AF.Copy)
                        v_ev[(c, half)] = rdop("scalar", [bk], vev, wait=[tok_mm, t_vone])
                v_done = v_ev[(NCH - 1, 1)]

                def csf(t):
                    for c in range(NCH):
                        t.matmul(bank[2][0:1, :], ones_cb[:], vaug[:, c, 0:512],
                                 start=(c == 0), stop=(c == NCH - 1))
                        t.matmul(bank[3][0:1, :], ones_cb[:], vaug[:, c, 512:1024],
                                 start=(c == 0), stop=(c == NCH - 1))
                        mm = t.matmul(bank[4][0:1, 0:16], ones_cb[:], vaug[:, c, 1024:1040],
                                      start=(c == 0), stop=(c == NCH - 1))
                    return mm
                cs_mm = mmop([2, 3, 4], csf, wait=[v_done, t_c2])
                def cse(sc):
                    sc.activation(out=epsv[0:1, 0:512], in_=bank[2][0:1, :], func=AF.Copy, scale=RATIO * EPS)
                    sc.activation(out=epsv[0:1, 512:1024], in_=bank[3][0:1, :], func=AF.Copy, scale=RATIO * EPS)
                    return sc.activation(out=epsv[0:1, 1024:1040], in_=bank[4][0:1, 0:16], func=AF.Copy,
                                         scale=RATIO * EPS)
                epsv_tok = rdop("scalar", [2, 3, 4], cse, wait=[cs_mm])

                prev_dgm = None
                prev_exp = None
                prev_aff = None
                prev_ctx = {}
                csx_toks = [None, None]
                prev_at = {0: None, 1: None}
                prev_ae = {0: None, 1: None}
                prev_amul = {0: None, 1: None}
                prev_amul2 = {}
                prev_db = {}
                seq_term = None

                for hp in range(8):
                    def proj_pair(w_sb, dst, hp=hp):
                        evs = []
                        for j in range(4):
                            bk = 6 + (j % 2)
                            def mmf(t, j=j, w_sb=w_sb, hp=hp, bk=bk):
                                for dc in range(8):
                                    mm = t.matmul(bank[bk][:], w_sb[:, dc, TS(hp, 128)],
                                                  xb[:, dc, TS(j, 512)], start=(dc == 0), stop=(dc == 7))
                                return mm
                            tok_mm = mmop([bk], mmf, wait=[t_xb, t_wk, t_wq])
                            evs.append(rdop("scalar", [bk],
                                            lambda sc, j=j, dst=dst, bk=bk: sc.activation(
                                                out=dst[:, TS(j, 512)], in_=bank[bk][:], func=AF.Copy, scale=DN),
                                            wait=[tok_mm]))
                        return evs

                    k_ev = proj_pair(wk_sb, kT2)
                    q_ev = proj_pair(wq_sb, qT2)

                    ctxev_toks = [None, None]
                    dgmA_toks = [None, None]
                    biasA_last = [None, None]
                    for h2 in range(2):
                        b0p = 64 * h2
                        hd = 2 * hp + h2
                        t_k2 = op("scalar",
                                  lambda sc, b0p=b0p: sc.activation(out=ksq[:], in_=kT2[b0p:b0p + 64, :],
                                                                    func=AF.Square),
                                  wait=[k_ev[-1], prev_dgm], inc=1)
                        ev_tok = [None] * NCH
                        for c in range(NCH):
                            dmm = mmop([c % 2],
                                       lambda t, c=c, b0p=b0p: t.matmul(
                                           bank[c % 2][:, 0:M], kT2[b0p:b0p + 64, TS(c, 128)],
                                           pj_sb[b0p:b0p + 64, :], start=True, stop=True),
                                       wait=[k_ev[c // 4], t_pj])
                            ev_tok[c] = rdop("vector", [c % 2],
                                             lambda v, c=c, h2=h2: v.tensor_copy(
                                                 out=dash2[:, h2, c, :], in_=bank[c % 2][:, 0:M]),
                                             wait=[dmm])
                        def dgmf(t):
                            for c in range(NCH):
                                mm = t.matmul(bank[6][:, c:c + 1], ksq[:, TS(c, 128)], half64[:],
                                              start=True, stop=True)
                            return mm
                        dgm = mmop([6], dgmf, wait=[t_k2, t_c3])
                        prev_dgm = dgm
                        dgmA_toks[h2] = dgm
                        dcopy = rdop("vector", [6],
                                     lambda v, h2=h2: v.tensor_copy(out=diagA_sb[:, h2, :], in_=bank[6][:, 0:NCH]),
                                     wait=[dgm])
                        st_all = op("vector",
                                    lambda v, h2=h2: v.reduce_max(stabacc[:, 2 * h2:2 * h2 + 1],
                                                                  dash2[:, h2, :, :], axis=AX.XY),
                                    wait=[ev_tok[NCH - 1]], inc=1)
                        tp = mmop([6],
                                  lambda t, h2=h2: t.transpose(bank[6][0:1, 32:160],
                                                               stabacc[:, 2 * h2:2 * h2 + 1],
                                                               id_f[:]),
                                  wait=[st_all, t_id])
                        nstr = rdop("vector", [6],
                                    lambda v, h2=h2: v.reduce_max(nstab_r[0:1, h2:h2 + 1], bank[6][0:1, 32:160],
                                                                  axis=AX.X, negate=True),
                                    wait=[tp])
                        nst = op("vector",
                                 lambda v, h2=h2: v.tensor_scalar_add(nstab[0:1, h2:h2 + 1], nstab_r[0:1, h2:h2 + 1],
                                                                      LNRATIO),
                                 wait=[nstr], inc=1)
                        nstbc = mmop([6],
                                     lambda t, h2=h2: t.matmul(bank[6][:, 20:21], ones_rf[:], nstab[0:1, h2:h2 + 1],
                                                               start=True, stop=True),
                                     wait=[nst, t_c1])
                        nstb_ev = rdop("vector", [6],
                                       lambda v, h2=h2: v.tensor_copy(out=nstab_b[:, h2:h2 + 1], in_=bank[6][:, 20:21]),
                                       wait=[nstbc])
                        bias_all = op("vector",
                                      lambda v, h2=h2: v.tensor_scalar(
                                          biasb[:, h2, :], diagA_sb[:, h2, :], -1.0,
                                          nstab_b[:, h2:h2 + 1], ALU.mult, ALU.add),
                                      wait=[dcopy, nstb_ev, prev_exp], inc=1)
                        exp_tok = [None] * NCH
                        ctx_tok = [None] * NCH
                        for c in range(NCH):
                            ew = [bias_all, prev_ctx.get(c % 4)]
                            exp_tok[c] = op("scalar",
                                            lambda sc, c=c, h2=h2: sc.activation(
                                                out=kp[c % 4][:], in_=dash2[:, h2, c, :], func=AF.Exp,
                                                bias=biasb[:, h2, c:c + 1]),
                                            wait=ew, inc=1)
                            def ctxf(t, c=c, hd=hd):
                                t.matmul(bank[2][0:128, 0:65], kp[c % 4][:, 0:128], vre[:, c, hd, :],
                                         start=(c == 0), stop=False)
                                t.matmul(bank[3][0:128, 0:65], kp[c % 4][:, 128:256], vre[:, c, hd, :],
                                         start=(c == 0), stop=False)
                                return t.matmul(bank[5][0:10, 0:65], kp[c % 4][:, 256:266], vre[:, c, hd, :],
                                                start=(c == 0), stop=False)
                            ctx_tok[c] = mmop([2, 3, 5] if c == 0 else [], ctxf,
                                              wait=[exp_tok[c], v_done])
                            prev_ctx[c % 4] = ctx_tok[c]
                        prev_exp = exp_tok[NCH - 1]
                        def ctxeps(t, hd=hd):
                            t.matmul(bank[2][0:128, 0:65], onesb_r[0:1, 0:128],
                                     epsv[0:1, hd * 65:(hd + 1) * 65], start=False, stop=True)
                            t.matmul(bank[3][0:128, 0:65], onesb_r[0:1, 0:128],
                                     epsv[0:1, hd * 65:(hd + 1) * 65], start=False, stop=True)
                            return t.matmul(bank[5][0:10, 0:65], onesb_r[0:1, 0:10],
                                            epsv[0:1, hd * 65:(hd + 1) * 65], start=False, stop=True)
                        ctx_fin = mmop([], ctxeps, wait=[ctx_tok[NCH - 1], epsv_tok, t_c7])
                        def ctxev_f(sc, h2=h2):
                            sc.activation(out=ctx_all[:, h2, 0, :], in_=bank[2][:, 0:65], func=AF.Copy)
                            sc.activation(out=ctx_all[:, h2, 1, :], in_=bank[3][:, 0:65], func=AF.Copy)
                            return sc.activation(out=ctx_all[0:10, h2, 2, :], in_=bank[5][0:10, 0:65], func=AF.Copy)
                        ctxev_toks[h2] = rdop("scalar", [2, 3, 5], ctxev_f, wait=[ctx_fin])
                        def csxf(t, h2=h2):
                            for mi, (m0, m1) in enumerate(MCH):
                                mm = t.matmul(bank[6][0:1, 200:265], ones_cb[0:m1 - m0, 0:1],
                                              ctx_all[0:m1 - m0, h2, mi, :], start=(mi == 0), stop=(mi == 2))
                            return mm
                        csx_mm = mmop([6], csxf, wait=[ctxev_toks[h2], t_c2])
                        csx_toks[h2] = rdop("scalar", [6],
                                            lambda sc, h2=h2: sc.activation(out=csx[0:1, h2, :],
                                                                            in_=bank[6][0:1, 200:265],
                                                                            func=AF.Copy, scale=RATIO * EPS),
                                            wait=[csx_mm])

                    for h2 in range(2):
                        b0p = 64 * h2
                        hd = 2 * hp + h2
                        t_q2 = op("scalar",
                                  lambda sc, b0p=b0p: sc.activation(out=ksq[:], in_=qT2[b0p:b0p + 64, :],
                                                                    func=AF.Square),
                                  wait=[q_ev[-1], prev_dgm], inc=1)
                        stq_tok = [None] * NCH
                        crow_tok = [None] * NCH
                        ct_toks = [None] * 4
                        ex_toks = {}
                        af_toks = {}
                        at_toks = [None] * 4
                        rec_toks = [None] * 4
                        last_af = [prev_amul[h2]]
                        dgq = None

                        def emit_front(j):
                            for ci in range(4):
                                c = 4 * j + ci
                                dq = mmop([c % 2],
                                          lambda t, c=c, b0p=b0p: t.matmul(
                                              bank[c % 2][:, 0:M], qT2[b0p:b0p + 64, TS(c, 128)],
                                              pj_sb[b0p:b0p + 64, :], start=True, stop=True),
                                          wait=[q_ev[c // 4], t_pj])
                                stq_tok[c] = rdop("vector", [c % 2],
                                                  lambda v, c=c, ci=ci, h2=h2: v.reduce_max(
                                                      scr4[:, h2, ci:ci + 1], bank[c % 2][:, 0:M], axis=AX.X),
                                                  wait=[dq])

                        def emit_mid(j):
                            ct_toks[j] = op("vector",
                                            lambda v, j=j, h2=h2: v.scalar_tensor_tensor(
                                                ctokb[:, h2, :], diagB_sb[:, h2, TS(j, 4)], -LNRATIO,
                                                scr4[:, h2, :], ALU.add, ALU.add),
                                            wait=[stq_tok[4 * j + 3], dqcopy,
                                                  crow_tok[4 * (j - 1) + 3] if j >= 1 else None], inc=1)
                            for ci in range(4):
                                c = 4 * j + ci
                                tpq = mmop([c % 2],
                                           lambda t, ci=ci, h2=h2: t.transpose(
                                               bank[c % 2][0:1, 300:428], ctokb[:, h2, ci:ci + 1], id_f[:]),
                                           wait=[ct_toks[j]])
                                crow_tok[c] = rdop("vector", [c % 2],
                                                   lambda v, c=c, ci=ci, h2=h2: v.tensor_copy(
                                                       out=crow2[0:1, h2, TS(ci, 128)],
                                                       in_=bank[c % 2][0:1, 300:428]),
                                                   wait=[tpq])

                        def emit_dT(j):
                            prev_af = last_af[0]
                            for mi, (m0, m1) in enumerate(MCH):
                                bk = 2 + (mi % 2)
                                def dTf(t, mi=mi, m0=m0, m1=m1, j=j, b0p=b0p, h2=h2, bk=bk):
                                    t.matmul(bank[bk][0:m1 - m0, :], pj_sb[b0p:b0p + 64, m0:m1],
                                             qT2[b0p:b0p + 64, TS(j, 512)], start=True, stop=False)
                                    return t.matmul(bank[bk][0:m1 - m0, :], negb[0:1, 0:m1 - m0],
                                                    crow2[0:1, h2, :], start=False, stop=True)
                                dT = mmop([bk], dTf, wait=[q_ev[j], crow_tok[4 * j + 3], t_c6])
                                ex = rdop("scalar", [bk],
                                          lambda sc, m0=m0, m1=m1, bk=bk: sc.activation(
                                              out=qpe[0:m1 - m0, :], in_=bank[bk][0:m1 - m0, :], func=AF.Exp),
                                          wait=[dT, prev_af])
                                af = op("vector",
                                        lambda v, mi=mi, m0=m0, m1=m1, h2=h2: v.tensor_scalar_add(
                                            qpt2[0:m1 - m0, h2, mi, :], qpe[0:m1 - m0, :], RATIO * EPS),
                                        wait=[ex, prev_at[h2]], inc=1)
                                ex_toks[(j, mi)] = ex
                                af_toks[(j, mi)] = af
                                prev_af = af
                                last_af[0] = af

                        def emit_at(j):
                            def attf(t, j=j, h2=h2):
                                for mi, (m0, m1) in enumerate(MCH):
                                    mm = t.matmul(bank[4][0:65, :], ctx_all[0:m1 - m0, h2, mi, :],
                                                  qpt2[0:m1 - m0, h2, mi, :], start=(mi == 0), stop=(mi == 2))
                                return mm
                            at = mmop([4], attf, wait=[af_toks[(j, 0)], af_toks[(j, 1)], af_toks[(j, 2)],
                                                       ctxev_toks[h2]])
                            at_toks[j] = at
                            prev_at[h2] = at
                            rec1 = rdop("scalar", [4],
                                        lambda sc, j=j, h2=h2: sc.activation(out=lnrow[0:1, :],
                                                                             in_=bank[4][64:65, :], func=AF.Ln),
                                        wait=[at])
                            rec_toks[j] = op("scalar",
                                             lambda sc, j=j, h2=h2: sc.activation(out=dinv2[0:1, h2, j % 2, :],
                                                                                  in_=lnrow[0:1, :], func=AF.Exp,
                                                                                  scale=-1.0),
                                             wait=[rec1, prev_db.get((h2, j % 2))], inc=1)
                            ae = rdop("scalar", [4],
                                      lambda sc, j=j, h2=h2: sc.activation(out=attf2[:, h2, j % 2, :],
                                                                           in_=bank[4][0:64, :], func=AF.Copy),
                                      wait=[at, rec1, prev_amul2.get((h2, j % 2))], inc=1)
                            prev_ae[h2] = ae
                            return ae

                        def emit_db(j, ae):
                            db = mmop([5],
                                      lambda t, j=j, h2=h2: t.matmul(bank[5][0:64, :], onesb_r[0:1, 0:64],
                                                                     dinv2[0:1, h2, j % 2, :],
                                                                     start=True, stop=True),
                                      wait=[rec_toks[j], t_c7])
                            prev_db[(h2, j % 2)] = db
                            amul = rdop("vector", [5],
                                        lambda v, j=j, hd=hd, b0p=b0p, h2=h2: v.tensor_mul(
                                            attnT[b0p:b0p + 64, hd // 2, TS(j, 512)], attf2[:, h2, j % 2, :],
                                            bank[5][0:64, :]),
                                        wait=[ae, db])
                            prev_amul2[(h2, j % 2)] = amul
                            prev_amul[h2] = amul
                            return amul

                        emit_front(0)
                        def dgqf(t):
                            for c in range(NCH):
                                mm = t.matmul(bank[7][:, c:c + 1], ksq[:, TS(c, 128)], half64[:],
                                              start=True, stop=True)
                            return mm
                        dgq = mmop([7], dgqf, wait=[t_q2, t_c3])
                        prev_dgm = dgq
                        dqcopy = rdop("vector", [7],
                                      lambda v, h2=h2: v.tensor_copy(out=diagB_sb[:, h2, :], in_=bank[7][:, 0:NCH]),
                                      wait=[dgq])
                        emit_mid(0)
                        emit_front(1)
                        emit_dT(0)
                        emit_mid(1)
                        ae0 = emit_at(0)
                        emit_front(2)
                        emit_dT(1)
                        emit_mid(2)
                        ae1 = emit_at(1)
                        emit_front(3)
                        emit_dT(2)
                        am0 = emit_db(0, ae0)
                        emit_mid(3)
                        ae2 = emit_at(2)
                        emit_dT(3)
                        am1 = emit_db(1, ae1)
                        ae3 = emit_at(3)
                        am2 = emit_db(2, ae2)
                        am3 = emit_db(3, ae3)
                        seq_term = am3

                P.barrier()
                inner1.close()

                t_wo = op("gpsimd", lambda g: g.dma_start(out=wvo_sb[:], in_=Wo.rearrange("(c p) f -> p c f", p=128)),
                          dma=True)
                innerC = ExitStack()
                _c = lambda nm, sh, dt: innerC.enter_context(nc.sbuf_tensor(f"{nm}_{s}", sh, dt))
                y_sb = _c("y_sb", [128, 8, 512], F32)
                xres = _c("xres", [128, 8, 512], F32)
                ybf = _c("ybf", [128, 8, 512], BF16)
                ysq = _c("ysq", [128, 8, 512], BF16)
                y1bt = _c("y1bt", [128, 8, 512], BF16)
                scr512 = _c("scr512", [128, 512], F32)
                rows = _c("rows", [1, 8, 512], F32)
                if True:
                    prev_apply = None
                    prev_dma = [None, None]
                    prev_y1c = None
                    for j in range(4):
                        t_xr = op("sync",
                                  lambda s_, s=s, j=j: s_.dma_start(
                                      out=xres[:], in_=xT[s].rearrange("(c p) t -> p c t", p=128)[:, :, TS(j, 512)]),
                                  wait=[prev_apply], dma=True)
                        ev_toks = []
                        for do_ in range(8):
                            pb = bank[do_ % 3]
                            def womm(t, do_=do_, j=j):
                                for dc in range(8):
                                    mm = t.matmul(pb[:] if False else bank[do_ % 3][:],
                                                  wvo_sb[:, dc, TS(do_, 128)], attnT[:, dc, TS(j, 512)],
                                                  start=(dc == 0), stop=(dc == 7))
                                return mm
                            ww = [t_wo] if (j == 0 and do_ == 0) else []
                            if do_ >= 3:
                                ww.append(ev_toks[do_ - 3])
                            elif j > 0:
                                ww.append(prev_apply)
                            wtok = op("tensor", womm, wait=ww, inc=1)
                            ew_ = [wtok, t_bo, t_xr]
                            if do_ == 0 and j > 0:
                                ew_ += [prev_dma[0], prev_y1c]
                            ev_toks.append(op("vector",
                                             lambda v, do_=do_: v.scalar_tensor_tensor(
                                                 y_sb[:, do_, :], bank[do_ % 3][:], bo_sb[:, do_:do_ + 1],
                                                 xres[:, do_, :], ALU.add, ALU.add),
                                             wait=ew_, inc=1))
                        t_add = ev_toks[-1]
                        t_ybf = op("vector", lambda v: v.tensor_copy(out=ybf[:], in_=y_sb[:]),
                                   wait=[t_add], inc=1)
                        t_ysq = op("scalar", lambda sc: sc.activation(out=ysq[:], in_=y_sb[:], func=AF.Square),
                                   wait=[t_add], inc=1)
                        def summ(t):
                            for dc in range(8):
                                mm = t.matmul(bank[3][0:1, :], ones_cb[:], ybf[:, dc, :],
                                              start=(dc == 0), stop=(dc == 7))
                            return mm
                        def sqmm(t):
                            for dc in range(8):
                                mm = t.matmul(bank[4][0:1, :], ones_cb[:], ysq[:, dc, :],
                                              start=(dc == 0), stop=(dc == 7))
                            return mm
                        sm_tok = op("tensor", summ, wait=[t_ybf, t_c2, prev_dma[0]], inc=1)
                        sq_tok = op("tensor", sqmm, wait=[t_ysq], inc=1)
                        mean_tok = op("vector",
                                      lambda v: v.tensor_scalar_mul(rows[0:1, 0, :], bank[3][0:1, :], 1.0 / D),
                                      wait=[sm_tok], inc=1)
                        var_tok = op("vector",
                                     lambda v: v.tensor_scalar_mul(rows[0:1, 1, :], bank[4][0:1, :], 1.0 / D),
                                     wait=[sq_tok], inc=1)
                        msq_tok = op("vector",
                                     lambda v: v.tensor_mul(rows[0:1, 3, :], rows[0:1, 0, :], rows[0:1, 0, :]),
                                     wait=[mean_tok], inc=1)
                        var2_tok = op("vector",
                                      lambda v: v.tensor_sub(rows[0:1, 1, :], rows[0:1, 1, :], rows[0:1, 3, :]),
                                      wait=[var_tok, msq_tok], inc=1)
                        ln_tok = op("scalar",
                                    lambda sc: sc.activation(out=rows[0:1, 3, :], in_=rows[0:1, 1, :],
                                                             func=AF.Ln, bias=eps_sb[:]),
                                    wait=[var2_tok, t_c4], inc=1)
                        rstd_tok = op("scalar",
                                      lambda sc: sc.activation(out=rows[0:1, 2, :], in_=rows[0:1, 3, :],
                                                               func=AF.Exp, scale=-0.5),
                                      wait=[ln_tok], inc=1)
                        mb_tok = op("tensor",
                                    lambda t: t.matmul(bank[5][:], ones_rf[:], rows[0:1, 0, :], start=True, stop=True),
                                    wait=[var2_tok, prev_apply], inc=1)
                        rb_tok = op("tensor",
                                    lambda t: t.matmul(bank[6][:], ones_rf[:], rows[0:1, 2, :], start=True, stop=True),
                                    wait=[rstd_tok, prev_apply], inc=1)
                        ap_toks = []
                        for dc in range(8):
                            t1 = op("vector",
                                    lambda v, dc=dc: v.tensor_sub(scr512[:], y_sb[:, dc, :], bank[5][:]),
                                    wait=[mb_tok] + ([] if dc else [rb_tok]), inc=1)
                            t2 = op("vector",
                                    lambda v, dc=dc: v.tensor_mul(y_sb[:, dc, :], scr512[:], bank[6][:]),
                                    wait=[t1], inc=1)
                            ap_toks.append(t2)
                        t_y1c = op("vector", lambda v: v.tensor_copy(out=y1bt[:], in_=y_sb[:]),
                                   wait=[ap_toks[-1], prev_dma[1]], inc=1)
                        gt0 = s * N + j * 512
                        d1 = op("sync",
                                lambda s_, gt0=gt0: s_.dma_start(
                                    out=y1f.rearrange("(c p) t -> p c t", p=128)[:, :, gt0:gt0 + 512], in_=y_sb[:]),
                                wait=[ap_toks[-1]], dma=True)
                        d2 = op("gpsimd",
                                lambda g, gt0=gt0: g.dma_start(
                                    out=y1bd.rearrange("(c p) t -> p c t", p=128)[:, :, gt0:gt0 + 512], in_=y1bt[:]),
                                wait=[t_y1c], dma=True)
                        prev_apply = ap_toks[-1]
                        prev_y1c = t_y1c
                        prev_dma = [d1, d2]
                    P.barrier()
                innerC.close()
                prev_seq_term = None
                innerC.close()
                prev_seq_term = None
            P.barrier()

        NT8 = NTOK // 512
        innerB = ExitStack()
        wTb = innerB.enter_context(nc.sbuf_tensor("wTb", [E, NTOK], BF16))
        with nc.sbuf_tensor("y1b", [128, 8, NTOK], BF16) as y1b:
            t_y1b = op("gpsimd",
                       lambda g: g.dma_start(out=y1b[:], in_=y1bd.rearrange("(c p) t -> p c t", p=128)),
                       dma=True)

            innerR = ExitStack()
            _r = lambda nm, sh, dt: innerR.enter_context(nc.sbuf_tensor(nm, sh, dt))
            lgT = _r("lgT", [8, 512], F32)
            lgtok = _r("lgtok", [128, 8], F32)
            m8 = _r("m8", [128, 8], F32)
            nv1 = _r("nv1", [128, 1], F32)
            maskb = _r("maskb", [128, 8], F32)
            expv = _r("expv", [128, 8], F32)
            mev = _r("mev", [128, 8], F32)
            zz = _r("zz", [128, 1], F32)
            zi = _r("zi", [128, 1], F32)
            wtok = _r("wtok", [128, 8], F32)
            if True:
                prev_wt = None
                ex_prev = None
                for jt in range(NT8):
                    def lgmm(t, jt=jt):
                        for dc in range(8):
                            mm = t.matmul(bank[0][0:8, :], wr_sb[:, dc, :], y1b[:, dc, TS(jt, 512)],
                                          start=(dc == 0), stop=(dc == 7))
                        return mm
                    lg_tok = op("tensor", lgmm, wait=[t_y1b, t_wr, prev_wt], inc=1)
                    lge_tok = op("vector", lambda v: v.tensor_copy(out=lgT[:], in_=bank[0][0:8, :]),
                                 wait=[lg_tok], inc=1)
                    for ci in range(4):
                        c = 4 * jt + ci
                        tp_tok = op("tensor",
                                    lambda t, ci=ci: t.matmul(bank[1][:, 0:8], lgT[0:8, TS(ci, 128)],
                                                              id_f[0:8, 0:8], start=True, stop=True),
                                    wait=[lge_tok] + ([prev_wt] if ci == 0 else []), inc=1)
                        ba_tok = op("vector",
                                    lambda v: v.tensor_add(lgtok[:], bank[1][:, 0:8], br_sb[:]),
                                    wait=[tp_tok, t_br, ex_prev], inc=1)
                        m8_tok = op("vector", lambda v: v.max(out=m8[:], in_=lgtok[:]), wait=[ba_tok], inc=1)
                        nv_tok = op("vector", lambda v: v.tensor_scalar_mul(nv1[:], m8[:, 0:1], -1.0),
                                    wait=[m8_tok, ex_prev], inc=1)
                        mk_tok = op("vector",
                                    lambda v: v.tensor_scalar(maskb[:], lgtok[:], m8[:, 1:2], None, ALU.is_ge),
                                    wait=[m8_tok], inc=1)
                        ex_tok = op("scalar",
                                    lambda sc: sc.activation(out=expv[:], in_=lgtok[:], func=AF.Exp, bias=nv1[:]),
                                    wait=[nv_tok], inc=1)
                        ex_prev = ex_tok
                        me_tok = op("vector", lambda v: v.tensor_mul(mev[:], maskb[:], expv[:]),
                                    wait=[mk_tok, ex_tok], inc=1)
                        z_tok = op("vector", lambda v: v.reduce_sum(zz[:], mev[:], axis=AX.X),
                                   wait=[me_tok], inc=1)
                        zi_tok = op("vector", lambda v: v.reciprocal(zi[:], zz[:]), wait=[z_tok], inc=1)
                        wt_tok = op("vector", lambda v: v.tensor_scalar_mul(wtok[:], mev[:], zi[:]),
                                    wait=[zi_tok], inc=1)
                        wtp_tok = op("tensor",
                                     lambda t: t.matmul(bank[2][0:8, 0:128], wtok[:], id_f[:], start=True, stop=True),
                                     wait=[wt_tok] + ([prev_wt] if ci == 0 else []), inc=1)
                        prev_wt = op("vector",
                                     lambda v, c=c: v.tensor_copy(out=wTb[0:8, TS(c, 128)], in_=bank[2][0:8, 0:128]),
                                     wait=[wtp_tok], inc=1)
            innerR.close()
            P.barrier()

            NCB = 4
            innerW1 = ExitStack()
            _w = lambda nm, sh, dt: innerW1.enter_context(nc.sbuf_tensor(nm, sh, dt))
            w1a = _w("w1a", [128, 2, 8, 512], BF16)
            w1g = _w("w1g", [128, 2, 8, 512], BF16)
            wbc = _w("wbc", [128, 8, 512], F32)
            asl = _w("asl", [128, 512], BF16)
            gw = _w("gw", [128, 512], BF16)
            actout = _w("actout", [128, 2, 4, 512], BF16)
            if True:
                w1_last_use = {}
                act_dma = {}
                prev_gw = None
                prev_asl = None
                prev_act = {}
                wbc_prev = None
                for e in range(E):
                    wb_toks = []
                    for tt in range(NT8):
                        wbm = op("tensor",
                                 lambda t, e=e, tt=tt: t.matmul(bank[4][:], oh_sb[0:8, TS(e, 128)],
                                                                wTb[0:8, TS(tt, 512)], start=True, stop=True),
                                 wait=[t_oh, prev_wt, wbc_prev] + ([wb_toks[-1]] if wb_toks else []), inc=1)
                        wb_toks.append(op("vector",
                                          lambda v, tt=tt: v.tensor_copy(out=wbc[:, tt, :], in_=bank[4][:]),
                                          wait=[wbm], inc=1))
                    wbc_done = wb_toks[-1]
                    wbc_prev = wbc_done
                    for cb in range(NCB):
                        gcb = e * NCB + cb
                        par = gcb % 2
                        dw = [w1_last_use.get(par)]
                        ta = op("gpsimd",
                                lambda g, e=e, cb=cb, par=par: g.dma_start(
                                    out=w1a[:, par], in_=W1[e][:, TS(cb, 512)].rearrange("(c p) f -> p c f", p=128)),
                                wait=dw, dma=True)
                        tg = op("gpsimd",
                                lambda g, e=e, cb=cb, par=par: g.dma_start(
                                    out=w1g[:, par], in_=W1[e][:, H + cb * 512:H + cb * 512 + 512]
                                    .rearrange("(c p) f -> p c f", p=128)),
                                dma=True)
                        for tt in range(NT8):
                            for jj in range(4):
                                i = tt * 4 + jj
                                ab = bank[0 + (i % 2)]
                                gb = bank[2 + (i % 2)]
                                jjg = cb * 4 + jj
                                def amm(t, par=par, jj=jj, tt=tt, ab=ab):
                                    for dc in range(8):
                                        mm = t.matmul(ab[:], w1a[:, par, dc, TS(jj, 128)],
                                                      y1b[:, dc, TS(tt, 512)], start=(dc == 0), stop=(dc == 7))
                                    return mm
                                aw = [ta if (tt == 0 and jj == 0) else None,
                                      prev_asl and prev_asl.get(i % 2)]
                                a_tok = op("tensor", amm, wait=[w for w in aw if w], inc=1)
                                def gmm(t, par=par, jj=jj, tt=tt, gb=gb):
                                    for dc in range(8):
                                        mm = t.matmul(gb[:], w1g[:, par, dc, TS(jj, 128)],
                                                      y1b[:, dc, TS(tt, 512)], start=(dc == 0), stop=(dc == 7))
                                    return mm
                                gw_ = [tg if (tt == 0 and jj == 0) else None,
                                       prev_gw and prev_gw.get(i % 2)]
                                g_tok = op("tensor", gmm, wait=[w for w in gw_ if w], inc=1)
                                if tt == NT8 - 1 and jj == 3:
                                    w1_last_use[par] = g_tok
                                sl_tok = op("scalar",
                                            lambda sc, e=e, jjg=jjg, ab=ab: sc.activation(
                                                out=asl[:], in_=ab[:], func=AF.Silu,
                                                bias=b1_sb[:, e, jjg:jjg + 1]),
                                            wait=[a_tok, t_b1, prev_act.get("asl")], inc=1)
                                gw_tok = op("vector",
                                            lambda v, e=e, jjg=jjg, tt=tt, gb=gb: v.scalar_tensor_tensor(
                                                gw[:], gb[:], b1_sb[:, e, 16 + jjg:16 + jjg + 1],
                                                wbc[:, tt, :], ALU.add, ALU.mult),
                                            wait=[g_tok, wbc_done, prev_act.get("gw")], inc=1)
                                ac_tok = op("vector",
                                            lambda v, jj=jj, tt=tt: v.tensor_mul(
                                                actout[:, tt % 2, jj, :], asl[:], gw[:]),
                                            wait=[sl_tok, gw_tok, act_dma.get(tt % 2)], inc=1)
                                prev_asl = prev_asl or {}
                                prev_gw = prev_gw or {}
                                prev_asl[i % 2] = sl_tok
                                prev_gw[i % 2] = gw_tok
                                prev_act["asl"] = ac_tok
                                prev_act["gw"] = ac_tok
                            act_dma[tt % 2] = op("sync",
                                                 lambda s_, e=e, cb=cb, tt=tt: s_.dma_start(
                                                     out=actd[e].rearrange("(c p) t -> p c t", p=128)
                                                     [:, TS(cb, 4), TS(tt, 512)],
                                                     in_=actout[:, tt % 2]),
                                                 wait=[prev_act["asl"]], dma=True)
                act_w1_done = act_dma[(NT8 - 1) % 2]
            innerW1.close()
            P.barrier()

        innerW2 = ExitStack()
        _w2 = lambda nm, sh, dt: innerW2.enter_context(nc.sbuf_tensor(nm, sh, dt))
        acc = _w2("acc", [128, 8, N], F32)
        w2b = _w2("w2b", [128, 2, 16, D], BF16)
        actin = _w2("actin", [128, 2, 16, 512], BF16)
        ybf2 = _w2("ybf2", [128, 8, 512], BF16)
        ysq2 = _w2("ysq2", [128, 8, 512], BF16)
        oscr = _w2("oscr", [128, 512], F32)
        rows2 = _w2("rows2", [1, 4, 512], F32)
        if True:
            for half in range(2):
                t_acc = op("sync",
                           lambda s_, half=half: s_.dma_start(
                               out=acc[:], in_=y1f.rearrange("(c p) t -> p c t", p=128)[:, :, TS(half, N)]),
                           wait=[], dma=True)
                w2_last = {}
                act_last = {}
                add_toks = {}
                for e in range(E):
                    par = e % 2
                    tw2 = op("gpsimd",
                             lambda g, e=e, par=par: g.dma_start(
                                 out=w2b[:, par], in_=W2[e].rearrange("(c p) f -> p c f", p=128)),
                             wait=[w2_last.get(par)], dma=True)
                    for tt in range(4):
                        gt = half * 4 + tt
                        apar = tt % 2
                        tact = op("gpsimd",
                                  lambda g, e=e, gt=gt, apar=apar: g.dma_start(
                                      out=actin[:, apar], in_=actd[e].rearrange("(c p) t -> p c t", p=128)
                                      [:, :, TS(gt, 512)]),
                                  wait=[act_last.get(apar), act_w1_done if (e == 0 and tt <= 1) else None],
                                  dma=True)
                        for dc in range(8):
                            i = tt * 8 + dc
                            ob = bank[i % 4]
                            def w2mm(t, e=e, gt=gt, dc=dc, apar=apar, par=par, ob=ob):
                                mm = t.matmul(ob[:], b2_sb[0:8, TS(dc, 128)], wTb[0:8, TS(gt, 512)],
                                              start=True, stop=False)
                                for h in range(16):
                                    mm = t.matmul(ob[:], w2b[:, par, h, TS(dc, 128)], actin[:, apar, h, :],
                                                  start=False, stop=(h == 15))
                                return mm
                            mm_tok = op("tensor", w2mm,
                                        wait=[tw2 if (tt == 0 and dc == 0) else None,
                                              tact if dc == 0 else None,
                                              t_b2 if (half == 0 and e == 0 and tt == 0 and dc == 0) else None,
                                              t_acc if (e == 0 and dc == 0) else None,
                                              add_toks.get(i % 4)], inc=1)
                            add_toks[i % 4] = op("vector",
                                                 lambda v, tt=tt, dc=dc, ob=ob: v.tensor_add(
                                                     acc[:, dc, TS(tt, 512)], acc[:, dc, TS(tt, 512)], ob[:]),
                                                 wait=[mm_tok, t_acc if e == 0 else None], inc=1)
                            if dc == 7:
                                act_last[apar] = add_toks[i % 4]
                            if tt == 3 and dc == 7:
                                w2_last[par] = add_toks[i % 4]
                lastadd = [add_toks[i] for i in range(4)]
                prev_out = None
                for tt in range(4):
                    t_ybf = op("vector",
                               lambda v, tt=tt: v.tensor_copy(out=ybf2[:], in_=acc[:].rearrange(
                                   "p c (a t) -> p c a t", t=512)[:, :, tt, :]),
                               wait=lastadd + [prev_out], inc=1)
                    t_ysq = op("scalar",
                               lambda sc, tt=tt: sc.activation(out=ysq2[:], in_=acc[:].rearrange(
                                   "p c (a t) -> p c a t", t=512)[:, :, tt, :], func=AF.Square),
                               wait=lastadd + [prev_out], inc=1)
                    def summ2(t):
                        for dc in range(8):
                            mm = t.matmul(bank[4][0:1, :], ones_cb[:], ybf2[:, dc, :],
                                          start=(dc == 0), stop=(dc == 7))
                        return mm
                    def sqmm2(t):
                        for dc in range(8):
                            mm = t.matmul(bank[5][0:1, :], ones_cb[:], ysq2[:, dc, :],
                                          start=(dc == 0), stop=(dc == 7))
                        return mm
                    sm_tok = op("tensor", summ2, wait=[t_ybf, prev_out], inc=1)
                    sq_tok = op("tensor", sqmm2, wait=[t_ysq], inc=1)
                    mean_tok = op("vector",
                                  lambda v: v.tensor_scalar_mul(rows2[0:1, 0, :], bank[4][0:1, :], 1.0 / D),
                                  wait=[sm_tok], inc=1)
                    sq2_tok = op("vector",
                                 lambda v: v.tensor_scalar_mul(rows2[0:1, 1, :], bank[5][0:1, :], 1.0 / D),
                                 wait=[sq_tok], inc=1)
                    msq_tok = op("vector",
                                 lambda v: v.tensor_mul(rows2[0:1, 3, :], rows2[0:1, 0, :], rows2[0:1, 0, :]),
                                 wait=[mean_tok], inc=1)
                    var_tok = op("vector",
                                 lambda v: v.tensor_sub(rows2[0:1, 1, :], rows2[0:1, 1, :], rows2[0:1, 3, :]),
                                 wait=[sq2_tok, msq_tok], inc=1)
                    ln_tok = op("scalar",
                                lambda sc: sc.activation(out=rows2[0:1, 3, :], in_=rows2[0:1, 1, :],
                                                         func=AF.Ln, bias=eps_sb[:]),
                                wait=[var_tok, t_c4], inc=1)
                    rstd_tok = op("scalar",
                                  lambda sc: sc.activation(out=rows2[0:1, 2, :], in_=rows2[0:1, 3, :],
                                                           func=AF.Exp, scale=-0.5),
                                  wait=[ln_tok], inc=1)
                    mb_tok = op("tensor",
                                lambda t: t.matmul(bank[6][:], ones_rf[:], rows2[0:1, 0, :], start=True, stop=True),
                                wait=[var_tok, prev_out], inc=1)
                    rb_tok = op("tensor",
                                lambda t: t.matmul(bank[7][:], ones_rf[:], rows2[0:1, 2, :], start=True, stop=True),
                                wait=[rstd_tok, prev_out], inc=1)
                    out_toks = []
                    for dc in range(8):
                        t1 = op("vector",
                                lambda v, tt=tt, dc=dc: v.tensor_sub(oscr[:], acc[:, dc, TS(tt, 512)], bank[6][:]),
                                wait=[mb_tok] + ([rb_tok] if dc == 0 else []), inc=1)
                        t2 = op("vector",
                                lambda v, tt=tt, dc=dc: v.tensor_mul(acc[:, dc, TS(tt, 512)], oscr[:], bank[7][:]),
                                wait=[t1], inc=1)
                        out_toks.append(t2)
                    dout = op("sync",
                              lambda s_, half=half, tt=tt: s_.dma_start(
                                  out=outT[half].rearrange("(c p) t -> p c t", p=128)[:, :, TS(tt, 512)],
                                  in_=acc[:].rearrange("p c (a t) -> p c a t", t=512)[:, :, tt, :]),
                              wait=[out_toks[-1]], dma=True)
                    prev_out = dout
                P.barrier()
        innerW2.close()
        innerB.close()

        fin1 = P.cnt["sync"]
        op("sync", lambda s_: s_.nop(), wait=[("sync", fin1)])
        fin2 = P.cnt["gpsimd"]
        op("gpsimd", lambda g: g.nop(), wait=[("gpsimd", fin2)])

        with nc.Block() as block:
            P.emit(block)
    return nc


_NC_CACHE = {}


def _get_nc():
    if "nc" not in _NC_CACHE:
        _NC_CACHE["nc"] = build_program()
    return _NC_CACHE["nc"]


def kernel(L0_attn_x, L0_attn_Wq, L0_attn_Wk, L0_attn_Wv, L0_attn_Wo, L0_attn_bo,
           L0_attn_proj, L0_ffn_W1, L0_ffn_b1, L0_ffn_W2, L0_ffn_b2, L0_ffn_Wr,
           L0_ffn_br):
    f32 = np.float32
    x = np.asarray(L0_attn_x, f32)
    xTt = np.ascontiguousarray(x.transpose(0, 2, 1))
    projT = np.ascontiguousarray(np.asarray(L0_attn_proj, f32).T)
    bo_l = np.ascontiguousarray(np.asarray(L0_attn_bo, f32).reshape(8, 128).T)
    b1_l = np.ascontiguousarray(np.asarray(L0_ffn_b1, f32).reshape(E, 32, 128).transpose(2, 0, 1))
    br_b = np.ascontiguousarray(np.broadcast_to(np.asarray(L0_ffn_br, f32), (128, E)))
    idf = np.eye(128, dtype=f32)
    oh8 = np.zeros((E, E * 128), dtype=ml_dtypes.bfloat16)
    for e in range(E):
        oh8[e, e * 128:(e + 1) * 128] = 1
    common = {
        "Wq": np.ascontiguousarray(np.asarray(L0_attn_Wq, f32)),
        "Wk": np.ascontiguousarray(np.asarray(L0_attn_Wk, f32)),
        "Wv": np.ascontiguousarray(np.asarray(L0_attn_Wv, f32)),
        "Wo": np.ascontiguousarray(np.asarray(L0_attn_Wo, f32)),
        "projT": projT, "bo_l": bo_l,
        "W1": np.ascontiguousarray(np.asarray(L0_ffn_W1, f32)),
        "b1_l": b1_l,
        "W2": np.ascontiguousarray(np.asarray(L0_ffn_W2, f32)),
        "b2": np.ascontiguousarray(np.asarray(L0_ffn_b2, f32)),
        "Wr": np.ascontiguousarray(np.asarray(L0_ffn_Wr, f32)),
        "br_b": br_b, "idf": idf, "oh8": oh8,
    }
    in_maps = []
    for i in range(8):
        m = dict(common)
        m["xT"] = np.ascontiguousarray(xTt[2 * i:2 * i + 2])
        in_maps.append(m)
    nc = _get_nc()
    res = run_bass_kernel_spmd(nc, in_maps, core_ids=list(range(8)))
    shards = [res.results[i]["outT"] for i in range(8)]
    out = np.concatenate(shards, axis=0)
    return np.ascontiguousarray(out.transpose(0, 2, 1)).astype(np.float32)
